# version 13
# baseline (speedup 1.0000x reference)
import contextlib
import numpy as np
import ml_dtypes
import concourse.bass as bass
import concourse.mybir as mybir
from concourse.bass_utils import run_bass_kernel_spmd

F32 = mybir.dt.float32
BF16 = mybir.dt.bfloat16
AF = mybir.ActivationFunctionType
ALU = mybir.AluOpType
AX = mybir.AxisListType

NCORES = 8
T = 4096
D = 1024
FF = 4096
HID = 2048
E = 8
HEADS = 16
DH = 64
SCALE_B = DH ** -0.5
COL = 512
NCOL = T // COL
DT = D // 128
HT = HID // 128
FT = FF // 128
EPS = 1e-5
PINIT = -10000.0

ENGS = ("sync", "gpsimd", "tensor", "vector", "scalar")


class Cx:

    def __init__(self, nc, es, blk):
        self.nc = nc
        self.es = es
        self.blk = blk
        self.sem = {}
        self.cnt = {}
        for name in ("wl", "xl", "so", "pe", "ve", "se", "mo", "mt"):
            self.sem[name] = es.enter_context(nc.semaphore(name))
            self.cnt[name] = 0
        self._streams = None

    def begin(self):
        self._streams = {e: [] for e in ENGS}

    def emit(self, eng, fn):
        self._streams[eng].append(fn)

    def step(self, eng, sem, waits, body, barrier=None):
        w = tuple((n, v) for n, v in waits)
        s = self.sem
        if barrier is None:
            barrier = eng in ("vector", "scalar", "gpsimd")
        def go(e):
            for nm, v in w:
                e.wait_ge(s[nm], v)
            last = body(e)
            if barrier:
                e.drain(fusable=False).then_inc(s[sem], 1)
            else:
                last.then_inc(s[sem], 1)
        self.cnt[sem] += 1
        self.emit(eng, go)
        return self.cnt[sem]

    def dma(self, eng, pairs, sem, waits=()):
        w = tuple(waits)
        s = self.sem
        def go(q):
            for nm, v in w:
                q.wait_ge(s[nm], v)
            for out, in_ in pairs:
                q.dma_start(out=out, in_=in_).then_inc(s[sem], 16)
        self.cnt[sem] += 16 * len(pairs)
        self.emit(eng, go)
        return self.cnt[sem]

    def dma_tr(self, eng, pairs, sem, waits=()):
        w = tuple(waits)
        s = self.sem
        def go(q):
            for nm, v in w:
                q.wait_ge(s[nm], v)
            for out, in_ in pairs:
                q.dma_start_transpose(out, in_).then_inc(s[sem], 16)
        self.cnt[sem] += 16 * len(pairs)
        self.emit(eng, go)
        return self.cnt[sem]

    def flush(self):
        for eng in ENGS:
            closures = self._streams[eng]
            if not closures:
                continue
            def body(e, closures=closures):
                for fn in closures:
                    fn(e)
            getattr(self.blk, eng)(body)
        self._streams = None


class Build:
    def __init__(self, phases=("a", "f1", "b", "f2")):
        self.phases = phases

    def declare(self, nc):
        self.x_in = nc.dram_tensor("x", [T, D], F32, kind="ExternalInput").ap()
        self.w = {}
        def win(name, shape):
            self.w[name] = nc.dram_tensor(name, list(shape), F32, kind="ExternalInput").ap()
        for n in ("ln0_g", "ln0_b", "ln1_g", "ln1_b", "time_decay", "time_first",
                  "tm_k", "tm_v", "tm_r", "cm_k", "cm_r", "gamma"):
            win(n, (D,))
        win("Wk", (D, D)); win("Wv", (D, D)); win("Wr", (D, D)); win("Wo", (D, D))
        win("Wck", (D, FF)); win("Wcv", (FF, D)); win("Wcr", (D, D))
        win("Wq", (D, D)); win("Wkv", (D, 2 * DH)); win("Wob", (D, D))
        win("router_w", (E, D))
        win("w_gate", (E, HID, D)); win("w_up", (E, HID, D)); win("w_down", (E, D, HID))

        self.y_out = nc.dram_tensor("y", [T, D], F32, kind="ExternalOutput").ap()
        self.xt_s = nc.dram_tensor("xt_s", [D, T], F32, kind="Internal").ap()
        self.wbf = {}
        for n, shape in (("Wk", (D, D)), ("Wv", (D, D)), ("Wr", (D, D)),
                         ("Wo", (D, D)), ("Wck", (D, FF)), ("Wcv", (FF, D)),
                         ("Wcr", (D, D)), ("Wq", (D, D)), ("Wkv", (D, 2 * DH)),
                         ("Wob", (D, D)), ("w_gate", (E, HID, D)),
                         ("w_up", (E, HID, D)), ("w_down", (E, D, HID))):
            self.wbf[n] = nc.dram_tensor(n + "_bf", list(shape), BF16, kind="Internal").ap()
        self.wguT = nc.dram_tensor("wguT", [E, D, 2 * HID], BF16, kind="Internal").ap()
        self.wdT = nc.dram_tensor("wdT", [E, 128, HT, D], BF16, kind="Internal").ap()
        self.rows_s = nc.dram_tensor("rows_s", [64, COL], F32, kind="Internal").ap()
        self.dbg = nc.dram_tensor("dbg", [64, COL], F32, kind="ExternalOutput").ap()

        self.c_ones = nc.inline_tensor(np.ones((128, 1), np.float32), name="c_ones").ap()
        self.c_idb = nc.inline_tensor(np.eye(128, dtype=ml_dtypes.bfloat16), name="c_idb").ap()
        self.c_idf = nc.inline_tensor(np.eye(128, dtype=np.float32), name="c_idf").ap()
        m01 = np.zeros((4, 128, COL), np.float32)
        for kc in range(4):
            j = np.arange(128)[:, None] + kc * 128
            q = np.arange(COL)[None, :]
            m01[kc] = (j <= q).astype(np.float32)
        self.c_mask01 = nc.inline_tensor(
            np.ascontiguousarray(m01.transpose(1, 0, 2).reshape(128, 4 * COL))
            .astype(ml_dtypes.bfloat16), name="c_mask01").ap()
        self.const_vals = [EPS, EPS * EPS, -0.5, 1.0 + EPS, -1.0, SCALE_B, 1.0, 0.0,
                           1.0 / D, -1e30, 0.5, 2.0]
        self.c_scal = nc.inline_tensor(
            np.array([self.const_vals], np.float32), name="c_scal").ap()

    def alloc_fixed(self, nc, es):
        def sb(name, shape, dtype):
            return es.enter_context(nc.sbuf_tensor(name, shape, dtype))[:, :]
        def psb(name, shape, dtype=F32):
            return es.enter_context(nc.psum_tensor(name, shape, dtype))[:, :]
        self.rowbuf = sb("rowbuf", [1, 9 * COL], F32)
        self.ones_sb = sb("ones_sb", [128, 1], F32)
        self.onesrow = sb("onesrow", [1, 128], F32)
        self.idb_sb = sb("idb_sb", [128, 128], BF16)
        self.idf_sb = sb("idf_sb", [128, 128], F32)
        self.cvec = sb("cvec", [128, 12 * DT], F32)
        self.negS = sb("negS", [1, E], F32)
        self.routerT = sb("routerT", [128, DT * E], F32)
        self.combT = sb("combT", [E, COL], F32)
        self.comb = sb("comb", [128, 4 * E], F32)
        self.stv = sb("stv", [128, 48], F32)
        self.const_sb = sb("const_sb", [128, len(self.const_vals)], F32)
        self.psA = psb("psA", [128, 7 * 512], F32)
        self.bank = [self.psA[:, i * 512:(i + 1) * 512] for i in range(7)]
        self.bankb = psb("bankb", [128, 1024], BF16)

    def row(self, i):
        return self.rowbuf[0:1, i * COL:(i + 1) * COL]

    def rown(self, i, n):
        return self.rowbuf[0:1, i * COL:i * COL + n]

    def bcast_rows_n(self, cx, items, waits, eng="gpsimd", sem="xl"):
        out_pairs = [(self.rows_s[slot:slot + 1, 0:n], srcap)
                     for srcap, slot, _, n in items]
        m = cx.dma(eng, out_pairs, sem, waits)
        in_pairs = [(dst, self.rows_s[slot:slot + 1, 0:n].partition_broadcast(128))
                    for _, slot, dst, n in items]
        return cx.dma(eng, in_pairs, sem, [(sem, m)])

    def bcast_rows(self, cx, items, waits, eng="gpsimd", sem="xl"):
        out_pairs = [(self.rows_s[slot:slot + 1, :], row) for row, slot, _ in items]
        m = cx.dma(eng, out_pairs, sem, waits)
        in_pairs = [(bc, self.rows_s[slot:slot + 1, :].partition_broadcast(128))
                    for _, slot, bc in items]
        return cx.dma(eng, in_pairs, sem, [(sem, m)])

    def ln_stats(self, cx, src, sq, fused_s12, waits):
        bk0, bk1 = self.bank[5], self.bank[6]
        def sq_body(e):
            last = None
            for dt in range(DT):
                last = e.activation(out=sq[dt], in_=src[dt], func=AF.Square)
            return last
        m_sq = cx.step("scalar", "se", waits, sq_body)
        def sum_body(e):
            for dt in range(DT):
                e.matmul(bk0[0:1, :], self.ones_sb[:, :], src[dt],
                         start=(dt == 0), stop=(dt == DT - 1))
            last = None
            for dt in range(DT):
                last = e.matmul(bk1[0:1, :], self.ones_sb[:, :], sq[dt],
                                start=(dt == 0), stop=(dt == DT - 1))
            return last
        m_sum = cx.step("tensor", "pe", list(waits) + [("se", m_sq), ("wl", self.m_fixed)], sum_body)
        def mv_body(e):
            e.tensor_scalar_mul(self.row(0), bk0[0:1, :], 1.0 / D)
            e.tensor_scalar_mul(self.row(1), bk1[0:1, :], 1.0 / D)
            e.tensor_tensor(self.row(6), self.row(0), self.row(0), ALU.mult)
            return e.tensor_tensor(self.row(1), self.row(1), self.row(6), ALU.subtract)
        m_mv = cx.step("vector", "ve", [("pe", m_sum)], mv_body,
                       barrier=False)
        def rstd_body(e):
            e.activation(out=self.row(6), in_=self.row(1), func=AF.Ln, bias=EPS)
            if fused_s12:
                e.activation(out=self.row(8), in_=self.row(1), func=AF.Ln,
                             bias=EPS * EPS, scale=1.0 + EPS)
            last = e.activation(out=self.row(2), in_=self.row(6), func=AF.Exp, scale=-0.5)
            if fused_s12:
                last = e.activation(out=self.row(4), in_=self.row(8), func=AF.Exp,
                                    scale=-0.5)
            return last
        m_rstd = cx.step("scalar", "se", [("ve", m_mv)], rstd_body,
                         barrier=False)
        def ms_body(e):
            last = e.tensor_tensor(self.row(3), self.row(0), self.row(2), ALU.mult)
            if fused_s12:
                last = e.tensor_tensor(self.row(5), self.row(0), self.row(4), ALU.mult)
            return last
        m_ms = cx.step("vector", "ve", [("se", m_rstd)], ms_body)
        return {"rstd": m_rstd, "ms": m_ms, "sumdone": m_sum}

    def transpose_in(self, cx, col, dst, stage, waits, eng="gpsimd", sem="xl"):
        c0 = col * COL
        pairs = [(stage[s], self.x_in[c0 + s * 128:c0 + (s + 1) * 128, :])
                 for s in range(4)]
        m_ld = cx.dma(eng, pairs, sem, waits)
        marks = []
        ve0 = self.cnt_ve(cx)
        for s in range(4):
            def tb(e, s=s):
                last = None
                for dt in range(DT):
                    last = e.transpose(
                        self.bank[3 + dt % 2][:, (dt // 2) * 128:(dt // 2) * 128 + 128],
                        stage[s][:, dt * 128:(dt + 1) * 128], self.idf_sb[:, :])
                return last
            w = [(sem, m_ld), ("wl", self.m_fixed)]
            if s >= 1:
                w.append(("ve", ve0 + s))
            marks.append(cx.step("tensor", "pe", w, tb))
        for s in range(4):
            def vb(e, s=s):
                last = None
                for dt in range(DT):
                    last = e.tensor_copy(
                        dst[dt][:, s * 128:(s + 1) * 128],
                        self.bank[3 + dt % 2][:, (dt // 2) * 128:(dt // 2) * 128 + 128])
                return last
            cx.step("vector", "ve", [("pe", marks[s])], vb)
        return cx.cnt["ve"]

    def cnt_ve(self, cx):
        return cx.cnt["ve"]

    def transpose_out(self, cx, col, src, stage, waits):
        c0 = col * COL
        marks = []
        ve0 = cx.cnt["ve"]
        for s in range(4):
            def tb(e, s=s):
                last = None
                for dt in range(DT):
                    last = e.transpose(
                        self.bank[3 + dt % 2][:, (dt // 2) * 128:(dt // 2) * 128 + 128],
                        src[dt][:, s * 128:(s + 1) * 128], self.idf_sb[:, :])
                return last
            w = list(waits) + [("wl", self.m_fixed)]
            if s >= 1:
                w.append(("ve", ve0 + s))
            marks.append(cx.step("tensor", "pe", w, tb))
        for s in range(4):
            def vb(e, s=s):
                last = None
                for dt in range(DT):
                    last = e.tensor_copy(
                        stage[s][:, dt * 128:(dt + 1) * 128],
                        self.bank[3 + dt % 2][:, (dt // 2) * 128:(dt // 2) * 128 + 128])
                return last
            mv = cx.step("vector", "ve", [("pe", marks[s])], vb)
            cx.dma("gpsimd",
                   [(self.y_out[c0 + s * 128:c0 + (s + 1) * 128, :], stage[s])],
                   "so", [("ve", mv)])
        return cx.cnt["so"]

    def load_col(self, cx, col, dst, waits, eng="gpsimd", sem="xl"):
        pairs = [(dst[dt], self.xt_s[dt * 128:(dt + 1) * 128, col * COL:(col + 1) * COL])
                 for dt in range(DT)]
        return cx.dma(eng, pairs, sem, waits)

    def store_col(self, cx, col, src, wait_ve, eng="gpsimd"):
        pairs = [(self.xt_s[dt * 128:(dt + 1) * 128, col * COL:(col + 1) * COL], src[dt])
                 for dt in range(DT)]
        return cx.dma(eng, pairs, "so", [("ve", wait_ve)])

    def slab_pair(self, buf, wsrc, c0, c1):
        nj = wsrc.shape[0] // 128
        cw = c1 - c0
        src3 = wsrc[:, c0:c1].rearrange("(j p) c -> p j c", p=128)
        dst3 = buf[:, 0:nj * cw].rearrange("p (j c) -> p j c", c=cw)
        return (dst3, src3)

    def preamble(self, cx, es):
        nc = self.nc
        cx.begin()
        cx.dma("sync", [(self.ones_sb[:, :], self.c_ones[:, :]),
                        (self.idb_sb[:, :], self.c_idb[:, :]),
                        (self.idf_sb[:, :], self.c_idf[:, :]),
                        (self.const_sb[:, :],
                         self.c_scal[0:1, :].partition_broadcast(128))], "wl")
        self.m_fixed = cx.cnt["wl"]
        for i, v in enumerate(self.const_vals):
            nc.const_aps.register(F32, float(v), self.const_sb[:, i:i + 1])
        def ones_init(e):
            return e.memset(self.onesrow[0:1, :], 1.0)
        m_or = cx.step("vector", "ve", [], ones_init)
        self.m_onesrow = m_or
        names = ("ln0_g", "ln0_b", "ln1_g", "ln1_b", "tm_k", "tm_v", "tm_r",
                 "cm_k", "cm_r", "time_decay", "time_first", "gamma")
        self.pv = {}
        pairs = []
        for i, n in enumerate(names):
            cols = []
            for dt in range(DT):
                ap = self.cvec[:, i * DT + dt:i * DT + dt + 1]
                src = self.w[n].rearrange("(a p) -> a p", p=128)[dt:dt + 1, :]
                pairs.append((ap, src.rearrange("a p -> p a")))
                cols.append(ap)
            self.pv[n] = cols
        self.m_cvec = cx.dma("gpsimd", pairs, "xl")
        kvr_w = ("Wk", "Wv", "Wr")
        rest_w = [n for n in self.wbf
                  if not n.startswith("w_") and n not in kvr_w]
        cx.dma("gpsimd", [(self.wbf[n][...], self.w[n][...]) for n in kvr_w], "xl")
        self.m_cast_kvr = cx.cnt["xl"]
        cx.dma("gpsimd", [(self.wbf[n][...], self.w[n][...]) for n in rest_w], "xl")
        self.m_cast_attn = cx.cnt["xl"]
        self.m_cast_moe = {}
        self.cast_jobs = [(n, h) for h in range(2)
                          for n in ("w_gate", "w_up", "w_down")]
        self.cast_state = {"pos": 0}
        router_es = contextlib.ExitStack()
        self.router_sb = router_es.enter_context(
            nc.sbuf_tensor("router_sb", [E, D], F32))[:, :]
        m_r = cx.dma("sync", [(self.router_sb[:, :], self.w["router_w"][...])], "wl")
        mk = []
        ve0 = cx.cnt["ve"]
        for dt in range(DT):
            def tb(e, dt=dt):
                return e.transpose(self.bank[dt % 2][:, 0:E],
                                   self.router_sb[:, dt * 128:(dt + 1) * 128],
                                   self.idf_sb[0:E, 0:E])
            w = [("wl", m_r), ("wl", self.m_fixed)]
            if dt >= 2:
                w.append(("ve", ve0 + dt - 1))
            mk.append(cx.step("tensor", "pe", w, tb))
        for dt in range(DT):
            def vb(e, dt=dt):
                return e.tensor_copy(self.routerT[:, dt * E:(dt + 1) * E],
                                     self.bank[dt % 2][:, 0:E])
            cx.step("vector", "ve", [("pe", mk[dt])], vb)
        m_rT = cx.cnt["ve"]
        def s_body(e):
            last = None
            for dt in range(DT):
                last = e.matmul(self.bank[2][0:1, 0:E], self.ones_sb[:, :],
                                self.routerT[:, dt * E:(dt + 1) * E],
                                start=(dt == 0), stop=(dt == DT - 1))
            return last
        m_s = cx.step("tensor", "pe", [("ve", m_rT)], s_body)
        def nS_body(e):
            return e.tensor_scalar_mul(self.negS[0:1, :], self.bank[2][0:1, 0:E], -1.0)
        cx.step("vector", "ve", [("pe", m_s)], nS_body)

        jobs = []
        for e_i in range(E):
            for n in ("w_gate", "w_up"):
                for dt in range(DT):
                    jobs.append((n, e_i, dt))
            for ht in range(HT):
                jobs.append(("w_down", e_i, ht))
        self.moet_jobs = jobs
        self.moet_state = {"pos": 0, "si": 0, "last": {0: 0, 1: 0}}
        if "a" not in self.phases:
            self.emit_cast(cx, len(self.cast_jobs))
            with nc.sbuf_tensor("stg_moe", [128, 2 * 2048], BF16,
                                side="right") as stg_th:
                self.stg_moe = stg_th[:, :]
                self.emit_moet(cx, len(jobs))
                cx.flush()
        else:
            cx.flush()
        router_es.close()

    def emit_cast(self, cx, k):
        st = self.cast_state
        while k > 0 and st["pos"] < len(self.cast_jobs):
            n, h = self.cast_jobs[st["pos"]]
            st["pos"] += 1
            k -= 1
            cx.dma("gpsimd",
                   [(self.wbf[n][4 * h:4 * h + 4, :, :],
                     self.w[n][4 * h:4 * h + 4, :, :])], "xl")
            self.m_cast_moe[(n, h)] = cx.cnt["xl"]
        if st["pos"] >= len(self.cast_jobs):
            self.m_cast = cx.cnt["xl"]

    def emit_moet(self, cx, n):
        st = self.moet_state
        stg = [self.stg_moe[:, 0:2048], self.stg_moe[:, 2048:4096]]
        while n > 0 and st["pos"] < len(self.moet_jobs):
            name, e_i, i = self.moet_jobs[st["pos"]]
            st["pos"] += 1
            n -= 1
            b = st["si"] % 2
            st["si"] += 1
            w = [("xl", self.m_cast_moe[(name, e_i // 4)])]
            if st["last"][b]:
                w.append(("mo", st["last"][b]))
            if name == "w_down":
                mt = cx.dma_tr("sync",
                               [(stg[b][:, 0:1024],
                                 self.wbf["w_down"][e_i, :, i * 128:(i + 1) * 128])],
                               "mt", w)
                st["last"][b] = cx.dma(
                    "sync", [(self.wdT[e_i, :, i, :], stg[b][:, 0:1024])],
                    "mo", [("mt", mt)])
            else:
                dstf = 0 if name == "w_gate" else 1
                mt = cx.dma_tr("sync",
                               [(stg[b][:, 0:2048],
                                 self.wbf[name][e_i, :, i * 128:(i + 1) * 128])],
                               "mt", w)
                st["last"][b] = cx.dma(
                    "sync",
                    [(self.wguT[e_i, i * 128:(i + 1) * 128,
                                dstf * HID:(dstf + 1) * HID], stg[b][:, 0:2048])],
                    "mo", [("mt", mt)])
        if st["pos"] >= len(self.moet_jobs):
            self.m_moeT = cx.cnt["mo"]

    def phase_ffn(self, cx, first_phase, last_phase):
        nc = self.nc
        self._ffn_n = getattr(self, "_ffn_n", 0) + 1
        tag = f"f{self._ffn_n}"
        so0 = cx.cnt["so"]
        with contextlib.ExitStack() as es:
            def sb(name, shape, dtype):
                return es.enter_context(nc.sbuf_tensor(tag + name, shape, dtype))[:, :]
            xc = [sb(f"_xc{d}", [128, COL], F32) for d in range(DT)]
            sqt = sb("_sq", [128, DT * COL], F32)
            sq = [sqt[:, d * COL:(d + 1) * COL] for d in range(DT)]
            stage = [sqt[:, s * 1024:(s + 1) * 1024] for s in range(4)]
            accb = [[sb(f"_ac{w}_{d}", [128, COL], BF16) for d in range(DT)]
                    for w in range(2)]
            hbf = [[sb(f"_hb{w}_{d}", [128, COL], BF16) for d in range(DT)]
                   for w in range(2)]
            crep = [sb(f"_cr{w}", [128, COL], F32) for w in range(2)]
            bc0 = crep[0]
            bc1 = crep[1]
            small = sb("_small", [128, 64], F32)
            silu = [sb(f"_si{h}", [128, COL], BF16) for h in range(2)]
            hebf = [sb(f"_he{h}", [128, COL], BF16) for h in range(HT)]
            wgu = [sb(f"_wg{d}", [128, 2 * HID], BF16) for d in range(DT)]
            wd = sb("_wd", [128, HT * D], BF16)
            bank = self.bank
            cx.begin()

            prev_gu_pe = 0
            prev_dn_pe = 0
            crep_free = {}
            for win in range(NCOL // 2):
                cols = (2 * win, 2 * win + 1)
                ser = [("so", cx.cnt["so"])]
                m_h = {}
                m_crows = {}
                for ci, col in enumerate(cols):
                    serc = list(ser)
                    if ci == 1:
                        serc += [("pe", cx.cnt["pe"]), ("ve", cx.cnt["ve"])]
                    if first_phase:
                        m_x = self.transpose_in(cx, col, xc, stage, serc)
                        waits_x = [("ve", m_x)]
                    else:
                        m_x = self.load_col(cx, col, xc, serc)
                        waits_x = [("xl", m_x)]
                    st = self.ln_stats(cx, xc, sq, False, waits_x)
                    m_bc = self.bcast_rows(cx, [(self.row(2), 0, bc0),
                                                (self.row(3), 1, bc1)],
                                           [("ve", st["ms"])])
                    def h_body(e, ci=ci):
                        last = None
                        for dt in range(DT):
                            e.tensor_tensor(sq[dt], xc[dt], bc0, ALU.mult)
                            last = e.tensor_tensor(hbf[ci][dt], sq[dt], bc1,
                                                   ALU.subtract)
                        return last
                    m_h[ci] = cx.step("vector", "ve", [("xl", m_bc)], h_body)

                    rstd4 = small[:, 0:4]
                    mk_rc = []
                    for ts in range(4):
                        def tb(e, ts=ts):
                            return e.transpose(bank[6][:, ts:ts + 1],
                                               self.rowbuf[0:1, 2 * COL + ts * 128:
                                                           2 * COL + ts * 128 + 128],
                                               self.idf_sb[0:1, 0:1])
                        w = [("se", st["rstd"])]
                        if ts == 0:
                            w.append(("ve", st["ms"]))
                        mk_rc.append(cx.step("tensor", "pe", w, tb))
                    def rc_ev(e):
                        return e.tensor_copy(rstd4, bank[6][:, 0:4])
                    m_rc = cx.step("vector", "ve", [("pe", mk_rc[-1])], rc_ev)

                    lgt = small[:, 8:16]
                    m1 = small[:, 16:17]; nm1 = small[:, 17:18]
                    t8 = small[:, 24:32]; m2 = small[:, 32:33]
                    gap = small[:, 33:34]; den = small[:, 34:35]
                    rden = small[:, 35:36]
                    num = small[:, 40:48]; msk = small[:, 48:56]
                    for ts in range(4):
                        def lg_body(e, ts=ts):
                            for dt in range(DT):
                                e.matmul(bank[4][:, 0:E],
                                         xc[dt][:, ts * 128:(ts + 1) * 128],
                                         self.routerT[:, dt * E:(dt + 1) * E],
                                         start=(dt == 0), stop=False)
                            return e.matmul(bank[4][:, 0:E],
                                            self.rowbuf[0:1, ts * 128:(ts + 1) * 128],
                                            self.negS[0:1, :], start=False, stop=True)
                        w = [("ve", m_rc)]
                        if ts > 0:
                            w.append(("ve", cx.cnt["ve"]))
                        m_lg = cx.step("tensor", "pe", w, lg_body)
                        def t2a(e, ts=ts):
                            e.drain(fusable=False)
                            e.tensor_scalar_mul(lgt, bank[4][:, 0:E],
                                                rstd4[:, ts:ts + 1])
                            e.drain(fusable=False)
                            e.tensor_reduce(m1, lgt, AX.X, ALU.max)
                            e.drain(fusable=False)
                            e.tensor_scalar_mul(nm1, m1, -1.0)
                            e.tensor_single_scalar(t8, lgt, m1[:, 0:1], ALU.is_ge)
                            e.drain(fusable=False)
                            e.scalar_tensor_tensor(t8, t8, -1e30, lgt, ALU.mult,
                                                   ALU.add)
                            e.drain(fusable=False)
                            e.tensor_reduce(m2, t8, AX.X, ALU.max)
                            e.drain(fusable=False)
                            return e.tensor_single_scalar(msk, lgt, m2[:, 0:1],
                                                          ALU.is_ge)
                        m_a = cx.step("vector", "ve", [("pe", m_lg)], t2a)
                        def t2s(e, ts=ts):
                            e.activation(out=num, in_=lgt, func=AF.Exp,
                                         bias=nm1[:, 0:1])
                            return e.activation(out=gap, in_=m2, func=AF.Exp,
                                                bias=nm1[:, 0:1])
                        m_b = cx.step("scalar", "se", [("ve", m_a)], t2s)
                        def t2c(e, ts=ts):
                            e.tensor_scalar_add(den, gap, 1.0)
                            e.tensor_tensor(num, num, msk, ALU.mult)
                            e.reciprocal(rden, den)
                            e.drain(fusable=False)
                            return e.tensor_scalar_mul(
                                self.comb[:, ts * E:(ts + 1) * E], num,
                                rden[:, 0:1])
                        cx.step("vector", "ve", [("se", m_b)], t2c)
                    mk_ct = []
                    for ts in range(4):
                        def tb(e, ts=ts):
                            return e.transpose(bank[6][0:E, ts * 128:(ts + 1) * 128],
                                               self.comb[:, ts * E:(ts + 1) * E],
                                               self.idf_sb[:, :])
                        mk_ct.append(cx.step("tensor", "pe",
                                             [("ve", cx.cnt["ve"])], tb))
                    def ct_ev(e):
                        last = None
                        for ts in range(4):
                            last = e.tensor_copy(self.combT[:, ts * 128:(ts + 1) * 128],
                                                 bank[6][0:E, ts * 128:(ts + 1) * 128])
                        return last
                    m_ct = cx.step("vector", "ve", [("pe", mk_ct[-1])], ct_ev)
                    m_crows[ci] = cx.dma(
                        "gpsimd",
                        [(self.rows_s[8 + 8 * ci + ei:9 + 8 * ci + ei, :],
                          self.combT[ei:ei + 1, :]) for ei in range(E)],
                        "xl", [("ve", m_ct)])

                for ei in range(E):
                    m_wgu_a = cx.dma("sync",
                                     [(wgu[dt], self.wguT[ei, dt * 128:(dt + 1) * 128, :])
                                      for dt in range(0, DT, 2)], "wl",
                                     [("mo", self.m_moeT), ("so", so0),
                                      ("pe", prev_gu_pe)])
                    m_wgu_b = cx.dma("gpsimd",
                                     [(wgu[dt], self.wguT[ei, dt * 128:(dt + 1) * 128, :])
                                      for dt in range(1, DT, 2)], "xl",
                                     [("mo", self.m_moeT), ("so", so0),
                                      ("pe", prev_gu_pe)])
                    m_wd = cx.dma("sync",
                                  [(wd[:, i * 2048:(i + 1) * 2048],
                                    self.wdT[ei, :, 2 * i:2 * i + 2, :]
                                    .rearrange("p a d -> p (a d)"))
                                   for i in range(HT // 2)], "wl",
                                  [("mo", self.m_moeT), ("pe", prev_dn_pe),
                                   ("so", so0)])
                    m_cr = {}
                    for ci in range(2):
                        m_cr[ci] = cx.dma(
                            "gpsimd",
                            [(crep[ci],
                              self.rows_s[8 + 8 * ci + ei:9 + 8 * ci + ei, :]
                              .partition_broadcast(128))],
                            "xl", [("xl", m_crows[ci]),
                                   ("ve", crep_free.get(ci, 0))])
                    for ci in range(2):
                        se0 = cx.cnt["se"]
                        ve0 = cx.cnt["ve"]
                        for ht in range(HT):
                            gb = bank[0 + (ht % 2)]
                            ub = bank[2 + (ht % 2)]
                            def g_body(e, ht=ht, gb=gb, ci=ci):
                                last = None
                                for dt in range(DT):
                                    last = e.matmul(gb,
                                                    wgu[dt][:, ht * 128:(ht + 1) * 128],
                                                    hbf[ci][dt], start=(dt == 0),
                                                    stop=(dt == DT - 1))
                                return last
                            w = [("wl", m_wgu_a), ("xl", m_wgu_b), ("ve", m_h[ci])]
                            if ht >= 2:
                                w += [("se", se0 + ht - 1), ("ve", ve0 + ht - 1)]
                            elif ci == 1 or ei > 0 or win > 0:
                                w.append(("se", cx.cnt["se"]))
                                w.append(("ve", cx.cnt["ve"]))
                            m_g = cx.step("tensor", "pe", w, g_body)
                            def u_body(e, ht=ht, ub=ub, ci=ci):
                                last = None
                                for dt in range(DT):
                                    last = e.matmul(
                                        ub,
                                        wgu[dt][:, HID + ht * 128:HID + (ht + 1) * 128],
                                        hbf[ci][dt], start=(dt == 0),
                                        stop=(dt == DT - 1))
                                return last
                            m_gu = cx.step("tensor", "pe", [("ve", m_h[ci])], u_body)
                            def si_body(e, ht=ht, gb=gb):
                                return e.activation(out=silu[ht % 2], in_=gb,
                                                    func=AF.Silu)
                            m_si = cx.step("scalar", "se", [("pe", m_g)], si_body,
                                            barrier=False)
                            def he_body(e, ht=ht, ub=ub):
                                return e.scalar_tensor_tensor(
                                    hebf[ht], ub, 1.0, silu[ht % 2],
                                    ALU.mult, ALU.mult)
                            cx.step("vector", "ve", [("se", m_si), ("pe", m_gu)],
                                    he_body)
                        m_he_all = cx.cnt["ve"]
                        if ci == 1:
                            prev_gu_pe = cx.cnt["pe"]
                        ve1 = cx.cnt["ve"]
                        for dtile in range(DT):
                            db = bank[4 + (dtile % 2)]
                            def dn_body(e, dtile=dtile, db=db):
                                last = None
                                for ht in range(HT):
                                    last = e.matmul(
                                        db,
                                        wd[:, ht * D + dtile * 128:ht * D + (dtile + 1) * 128],
                                        hebf[ht], start=(ht == 0), stop=(ht == HT - 1))
                                return last
                            w = [("wl", m_wd), ("ve", m_he_all)]
                            if dtile >= 2:
                                w.append(("ve", ve1 + dtile - 1))
                            m_dn = cx.step("tensor", "pe", w, dn_body)
                            def ac_body(e, dtile=dtile, db=db, ei=ei, ci=ci):
                                if ei == 0:
                                    return e.scalar_tensor_tensor(
                                        accb[ci][dtile], db, 1.0, crep[ci],
                                        ALU.mult, ALU.mult)
                                e.scalar_tensor_tensor(sq[dtile], db, 1.0,
                                                       crep[ci],
                                                       ALU.mult, ALU.mult)
                                e.drain(fusable=False)
                                return e.tensor_tensor(accb[ci][dtile],
                                                       accb[ci][dtile],
                                                       sq[dtile], ALU.add)
                            cx.step("vector", "ve", [("pe", m_dn), ("xl", m_cr[ci])],
                                    ac_body)
                        if ci == 1:
                            prev_dn_pe = cx.cnt["pe"]
                        crep_free[ci] = cx.cnt["ve"]
                for ci, col in enumerate(cols):
                    m_x2 = self.load_col(cx, col, xc,
                                         [("pe", cx.cnt["pe"]), ("ve", cx.cnt["ve"])])
                    def res_body(e, ci=ci):
                        last = None
                        for dt in range(DT):
                            last = e.tensor_tensor(xc[dt], xc[dt], accb[ci][dt],
                                                   ALU.add)
                        return last
                    m_res = cx.step("vector", "ve", [("xl", m_x2)], res_body)
                    if last_phase:
                        self.transpose_out(cx, col, xc, stage, [("ve", m_res)])
                    else:
                        self.store_col(cx, col, xc, m_res)
            cx.flush()

    def phase_attn_a(self, cx, first_phase, last_phase):
        nc = self.nc
        so0 = cx.cnt["so"]
        with contextlib.ExitStack() as es:
            def sb(name, shape, dtype):
                return es.enter_context(nc.sbuf_tensor(name, shape, dtype))[:, :]
            xc = [sb(f"a_xc{d}", [128, COL], F32) for d in range(DT)]
            h1 = [sb(f"a_h1{d}", [128, COL], F32) for d in range(DT)]
            xa = [sb(f"a_xa{d}", [128, COL], F32) for d in range(DT)]
            rr = [sb(f"a_rr{d}", [128, COL], F32) for d in range(DT)]
            sqt = sb("a_sq", [128, DT * COL], F32)
            sq = [sqt[:, d * COL:(d + 1) * COL] for d in range(DT)]
            stage = [sqt[:, s * 1024:(s + 1) * 1024] for s in range(4)]
            kvx = sqt[:, 0:515]
            khx = sqt[:, 515:1030]
            Ax = sqt[:, 1030:1545]
            Bx = sqt[:, 1545:2060]
            dec2 = sqt[:, 2060:2316]
            k_f = sqt[:, 2316:2828]
            v_f = sqt[:, 2828:3340]
            bc = [sb(f"a_bc{i}", [128, COL], F32) for i in range(4)]
            xs_t = sb("a_xs", [128, COL], F32)
            d1_t = sb("a_d1", [128, COL], F32)
            wk_sc = sb("a_wksc", [128, 20], F32)
            ew = [wk_sc[:, 8 + d:9 + d] for d in range(DT)]
            eu_t = sb("a_eu", [128, DT], F32)
            eu = [eu_t[:, d:d + 1] for d in range(DT)]
            mixk = [sb(f"a_mk{d}", [128, COL], BF16) for d in range(DT)]
            mixv = [sb(f"a_mv{d}", [128, COL], BF16) for d in range(DT)]
            mixr = [sb(f"a_mr{d}", [128, COL], BF16) for d in range(DT)]
            wkvb = [sb(f"a_wk{d}", [128, COL], BF16) for d in range(DT)]
            mixck = [sb(f"a_mc{d}", [128, COL], BF16) for d in range(DT)]
            mixcr = [sb(f"a_mq{d}", [128, COL], BF16) for d in range(DT)]
            kk2t = sb("a_kk2", [128, (FT - 16) * COL], BF16)
            kk2 = (mixk + mixv +
                   [kk2t[:, f * COL:(f + 1) * COL] for f in range(FT - 16)])
            wbuf = sb("a_wbuf", [128, 2 * 3072], BF16)
            wslab = [wbuf[:, 0:3072], wbuf[:, 3072:6144]]
            self.stg_moe = es.enter_context(
                nc.sbuf_tensor("stg_moe", [128, 2 * 2048], BF16,
                               side="right"))[:, :]
            A_s = [self.stv[:, i:i + 1] for i in range(0, 8)]
            B_s = [self.stv[:, i:i + 1] for i in range(8, 16)]
            P_s = [self.stv[:, i:i + 1] for i in range(16, 24)]
            csh = [self.stv[:, i:i + 1] for i in range(24, 32)]
            csh2 = [self.stv[:, i:i + 1] for i in range(32, 40)]
            km = wk_sc[:, 0:1]; Mt = wk_sc[:, 1:2]; negM = wk_sc[:, 2:3]
            dP = wk_sc[:, 3:4]; edP = wk_sc[:, 4:5]; iA = wk_sc[:, 5:6]
            iB = wk_sc[:, 6:7]; lnB = wk_sc[:, 7:8]
            iA2 = wk_sc[:, 16:17]; iB2 = wk_sc[:, 17:18]
            bank = self.bank
            cx.begin()

            def init_v(e):
                for dt in range(DT):
                    e.memset(A_s[dt], 0.0)
                    e.memset(B_s[dt], 0.0)
                    e.memset(P_s[dt], PINIT)
                    e.memset(csh[dt], 0.0)
                    e.memset(csh2[dt], 0.0)
                return e.memset(dec2, 1.0)
            m_iv = cx.step("vector", "ve", [], init_v)
            def init_s(e):
                for dt in range(DT):
                    e.activation(out=ew[dt], in_=self.pv["time_decay"][dt], func=AF.Exp)
                for dt in range(DT):
                    e.activation(out=ew[dt], in_=ew[dt], func=AF.Exp, scale=-1.0)
                last = None
                for dt in range(DT):
                    last = e.activation(out=eu[dt], in_=self.pv["time_first"][dt],
                                        func=AF.Exp)
                return last
            m_is = cx.step("scalar", "se", [("xl", self.m_cvec)], init_s)

            wpar = 0
            wfree = {0: 0, 1: 0}
            for col in range(NCOL):
                ser = [("so", cx.cnt["so"])]
                if first_phase:
                    m_x = self.transpose_in(cx, col, xc, stage, ser)
                    waits_x = [("ve", m_x)]
                else:
                    m_x = self.load_col(cx, col, xc, ser)
                    waits_x = [("xl", m_x)]
                st = self.ln_stats(cx, xc, sq, True, waits_x)
                m_bc = self.bcast_rows(cx,
                                       [(self.row(2), 0, bc[0]), (self.row(3), 1, bc[1]),
                                        (self.row(4), 2, bc[2]), (self.row(5), 3, bc[3])],
                                       [("ve", st["ms"])])
                def hx_body(e):
                    last = None
                    for dt in range(DT):
                        e.tensor_tensor(h1[dt], xc[dt], bc[0], ALU.mult)
                        e.tensor_tensor(h1[dt], h1[dt], bc[1], ALU.subtract)
                        e.tensor_tensor(sq[dt], xc[dt], bc[2], ALU.mult)
                        e.tensor_tensor(sq[dt], sq[dt], bc[3], ALU.subtract)
                        last = e.tensor_scalar(xa[dt], sq[dt],
                                               self.pv["ln0_g"][dt][:, 0:1],
                                               self.pv["ln0_b"][dt][:, 0:1],
                                               ALU.mult, ALU.add)
                    return last
                m_hx = cx.step("vector", "ve", [("xl", m_bc)], hx_body,
                               barrier=False)
                def mix_body(e):
                    last = None
                    for dt in range(DT):
                        e.tensor_copy(xs_t[:, 1:COL], xa[dt][:, 0:COL - 1])
                        e.tensor_copy(xs_t[:, 0:1], csh[dt])
                        e.tensor_copy(csh[dt], xa[dt][:, COL - 1:COL])
                        e.tensor_tensor(d1_t, xa[dt], xs_t, ALU.subtract)
                        for name, dstl in (("tm_k", mixk), ("tm_v", mixv), ("tm_r", mixr)):
                            last = e.scalar_tensor_tensor(
                                dstl[dt], d1_t, self.pv[name][dt][:, 0:1], xs_t,
                                ALU.mult, ALU.add)
                    return last
                m_mix = cx.step("vector", "ve", [("ve", m_hx)], mix_body)
                if col == 0 and getattr(self, "debug", False):
                    cx.dma("gpsimd", [(self.dbg[0:1, :], h1[0][0:1, :]),
                                      (self.dbg[1:2, :], xa[0][0:1, :]),
                                      (self.dbg[2:3, :], mixk[0][0:1, :]),
                                      (self.dbg[3:4, :], mixr[0][0:1, :]),
                                      (self.dbg[30:31, :], bc[0][0:1, :]),
                                      (self.dbg[31:32, :], bc[2][0:1, :])], "so",
                           [("ve", m_mix)])

                for dt2 in range(DT):
                    b = wpar % 2; wpar += 1
                    def mkpairs(dt2=dt2, b=b):
                        ps = []
                        for i, n in enumerate(("Wk", "Wv", "Wr")):
                            ps.append(self.slab_pair(
                                wslab[b][:, i * 1024:(i + 1) * 1024],
                                self.wbf[n], dt2 * 128, (dt2 + 1) * 128))
                        return ps
                    m_w = cx.dma("sync", mkpairs(), "wl",
                                 [("xl", self.m_cast_kvr), ("pe", wfree[b]),
                                  ("so", so0)])
                    kb = bank[0]; vb = bank[1]; rb = bank[2]
                    def kvr_body(e, dt2=dt2, b=b):
                        for j in range(DT):
                            e.matmul(kb, wslab[b][:, j * 128:(j + 1) * 128],
                                     mixk[j], start=(j == 0), stop=(j == DT - 1))
                        for j in range(DT):
                            e.matmul(vb, wslab[b][:, 1024 + j * 128:1024 + (j + 1) * 128],
                                     mixv[j], start=(j == 0), stop=(j == DT - 1))
                        last = None
                        for j in range(DT):
                            last = e.matmul(rb,
                                            wslab[b][:, 2048 + j * 128:2048 + (j + 1) * 128],
                                            mixr[j], start=(j == 0), stop=(j == DT - 1))
                        return last
                    w = [("wl", m_w), ("ve", m_mix)]
                    if dt2 >= 1:
                        w.append(("ve", prev_wkv1))
                        w.append(("se", prev_evr))
                    m_kvr = cx.step("tensor", "pe", w, kvr_body)
                    wfree[b] = m_kvr
                    def ev_r(e, dt2=dt2):
                        return e.activation(out=rr[dt2], in_=rb, func=AF.Sigmoid)
                    m_evr = cx.step("scalar", "se", [("pe", m_kvr)], ev_r,
                                    barrier=False)
                    prev_evr = m_evr
                    def wkv1(e, dt2=dt2):
                        e.tensor_copy(k_f, kb)
                        e.tensor_copy(v_f, vb)
                        e.tensor_reduce(km, k_f, AX.X, ALU.max)
                        e.drain(fusable=False)
                        e.tensor_tensor(Mt, km, P_s[dt2], ALU.max)
                        e.drain(fusable=False)
                        e.tensor_scalar_mul(negM, Mt, -1.0)
                        e.tensor_tensor(dP, P_s[dt2], Mt, ALU.subtract)
                        e.memset(dec2, 1.0)
                        e.drain(fusable=False)
                        return e.tensor_scalar_mul(dec2, dec2, ew[dt2][:, 0:1])
                    m_w1 = cx.step("vector", "ve", [("pe", m_kvr), ("se", m_is)], wkv1,
                                   barrier=False)
                    prev_wkv1 = m_w1
                    def wkv_s(e, dt2=dt2):
                        e.activation(out=khx[:, 1:256], in_=k_f[:, 0:255], func=AF.Exp,
                                     bias=negM[:, 0:1])
                        e.activation(out=khx[:, 257:512], in_=k_f[:, 255:510],
                                     func=AF.Exp, bias=negM[:, 0:1])
                        e.activation(out=khx[:, 513:515], in_=k_f[:, 510:512],
                                     func=AF.Exp, bias=negM[:, 0:1])
                        return e.activation(out=edP, in_=dP, func=AF.Exp)
                    m_ws = cx.step("scalar", "se", [("ve", m_w1)], wkv_s,
                                   barrier=False)
                    def wkv2a(e, dt2=dt2):
                        e.tensor_tensor(kvx[:, 1:256], khx[:, 1:256],
                                        v_f[:, 0:255], ALU.mult)
                        e.tensor_tensor(kvx[:, 257:512], khx[:, 257:512],
                                        v_f[:, 255:510], ALU.mult)
                        e.tensor_tensor(kvx[:, 513:515], khx[:, 513:515],
                                        v_f[:, 510:512], ALU.mult)
                        e.tensor_tensor(iA, A_s[dt2], edP, ALU.mult)
                        e.tensor_tensor(iB, B_s[dt2], edP, ALU.mult)
                        e.drain(fusable=False)
                        e.tensor_copy(kvx[:, 0:1], iA)
                        e.tensor_copy(khx[:, 0:1], iB)
                        e.drain(fusable=False)
                        e.tensor_tensor_scan(out=Ax[:, 0:256], data0=dec2[:, 0:256],
                                             data1=kvx[:, 0:256], initial=0.0,
                                             op0=ALU.mult, op1=ALU.add)
                        e.tensor_tensor_scan(out=Bx[:, 0:256], data0=dec2[:, 0:256],
                                             data1=khx[:, 0:256], initial=0.0,
                                             op0=ALU.mult, op1=ALU.add)
                        e.drain(fusable=False)
                        e.tensor_copy(kvx[:, 256:257], Ax[:, 255:256])
                        e.tensor_copy(khx[:, 256:257], Bx[:, 255:256])
                        e.drain(fusable=False)
                        e.tensor_tensor_scan(out=Ax[:, 256:512], data0=dec2[:, 0:256],
                                             data1=kvx[:, 256:512], initial=0.0,
                                             op0=ALU.mult, op1=ALU.add)
                        e.tensor_tensor_scan(out=Bx[:, 256:512], data0=dec2[:, 0:256],
                                             data1=khx[:, 256:512], initial=0.0,
                                             op0=ALU.mult, op1=ALU.add)
                        e.drain(fusable=False)
                        e.tensor_copy(kvx[:, 512:513], Ax[:, 511:512])
                        e.tensor_copy(khx[:, 512:513], Bx[:, 511:512])
                        e.drain(fusable=False)
                        e.tensor_tensor_scan(out=Ax[:, 512:515], data0=dec2[:, 0:3],
                                             data1=kvx[:, 512:515], initial=0.0,
                                             op0=ALU.mult, op1=ALU.add)
                        return e.tensor_tensor_scan(out=Bx[:, 512:515],
                                             data0=dec2[:, 0:3],
                                             data1=khx[:, 512:515], initial=0.0,
                                             op0=ALU.mult, op1=ALU.add)
                    m_w2a = cx.step("vector", "ve", [("se", m_ws)], wkv2a,
                                    barrier=False)
                    def cvcc(e, dt2=dt2):
                        e.activation(out=v_f[:, 0:255], in_=kvx[:, 1:256],
                                     func=AF.Copy, scale=eu[dt2][:, 0:1])
                        e.activation(out=v_f[:, 255:510], in_=kvx[:, 257:512],
                                     func=AF.Copy, scale=eu[dt2][:, 0:1])
                        e.activation(out=v_f[:, 510:512], in_=kvx[:, 513:515],
                                     func=AF.Copy, scale=eu[dt2][:, 0:1])
                        e.activation(out=k_f[:, 0:255], in_=khx[:, 1:256],
                                     func=AF.Copy, scale=eu[dt2][:, 0:1])
                        e.activation(out=k_f[:, 255:510], in_=khx[:, 257:512],
                                     func=AF.Copy, scale=eu[dt2][:, 0:1])
                        return e.activation(out=k_f[:, 510:512], in_=khx[:, 513:515],
                                            func=AF.Copy, scale=eu[dt2][:, 0:1])
                    m_cvcc = cx.step("scalar", "se", [("ve", m_w2a)], cvcc,
                                     barrier=False)
                    def wkv2b(e, dt2=dt2):
                        e.drain(fusable=False)
                        e.tensor_tensor(kvx[:, 0:255], Ax[:, 0:255], v_f[:, 0:255],
                                        ALU.add)
                        e.tensor_tensor(kvx[:, 255:510], Ax[:, 256:511],
                                        v_f[:, 255:510], ALU.add)
                        e.tensor_tensor(kvx[:, 510:512], Ax[:, 512:514],
                                        v_f[:, 510:512], ALU.add)
                        e.tensor_tensor(khx[:, 0:255], Bx[:, 0:255], k_f[:, 0:255],
                                        ALU.add)
                        e.tensor_tensor(khx[:, 255:510], Bx[:, 256:511],
                                        k_f[:, 255:510], ALU.add)
                        e.tensor_tensor(khx[:, 510:512], Bx[:, 512:514],
                                        k_f[:, 510:512], ALU.add)
                        e.drain(fusable=False)
                        e.reciprocal(v_f, khx[:, 0:512])
                        e.drain(fusable=False)
                        e.tensor_tensor(k_f, kvx[:, 0:512], v_f, ALU.mult)
                        e.tensor_copy(iB, Bx[:, 514:515])
                        e.tensor_copy(iA, Ax[:, 514:515])
                        e.drain(fusable=False)
                        e.tensor_tensor(wkvb[dt2], k_f, rr[dt2], ALU.mult)
                        return e.reciprocal(km, iB)
                    m_w2 = cx.step("vector", "ve", [("se", m_cvcc)], wkv2b)
                    if col == 0 and dt2 == 0 and getattr(self, "debug", False):
                        cx.dma("gpsimd", [(self.dbg[5:6, :], k_f[0:1, :]),
                                          (self.dbg[6:7, :], v_f[0:1, :]),
                                          (self.dbg[8:9, :], Ax[0:1, 0:512]),
                                          (self.dbg[9:10, :], Bx[0:1, 0:512]),
                                          (self.dbg[10:11, :], wkvb[0][0:1, :]),
                                          (self.dbg[13:14, 0:16], wk_sc[0:1, 0:16])], "so",
                               [("ve", m_w2)])
                    def wkv_s2(e, dt2=dt2):
                        return e.activation(out=lnB, in_=iB, func=AF.Ln)
                    m_ws2 = cx.step("scalar", "se", [("ve", m_w2)], wkv_s2,
                                    barrier=False)
                    def wkv3(e, dt2=dt2):
                        e.tensor_tensor(A_s[dt2], iA, km, ALU.mult)
                        e.memset(B_s[dt2], 1.0)
                        return e.tensor_tensor(P_s[dt2], Mt, lnB, ALU.add)
                    cx.step("vector", "ve", [("se", m_ws2)], wkv3, barrier=False)

                m_wkv_all = cx.cnt["ve"]
                for dt2 in range(DT):
                    b = wpar % 2; wpar += 1
                    m_w = cx.dma("sync",
                                 [self.slab_pair(wslab[b][:, 0:1024],
                                                 self.wbf["Wo"], dt2 * 128,
                                                 (dt2 + 1) * 128)], "wl",
                                 [("xl", self.m_cast_attn), ("pe", wfree[b]),
                                  ("so", so0)])
                    ob = bank[3 + dt2 % 2]
                    def wo_body(e, dt2=dt2, b=b, ob=ob):
                        last = None
                        for j in range(DT):
                            last = e.matmul(ob, wslab[b][:, j * 128:(j + 1) * 128],
                                            wkvb[j], start=(j == 0), stop=(j == DT - 1))
                        return last
                    w = [("wl", m_w), ("ve", m_wkv_all)]
                    if dt2 >= 2:
                        w.append(("ve", cx.cnt["ve"]))
                    m_wo = cx.step("tensor", "pe", w, wo_body)
                    wfree[b] = m_wo
                    def y1_body(e, dt2=dt2, ob=ob):
                        return e.tensor_tensor(h1[dt2], h1[dt2], ob, ALU.add)
                    cx.step("vector", "ve", [("pe", m_wo)], y1_body)
                m_y1 = cx.cnt["ve"]
                if col == 0 and getattr(self, "debug", False):
                    cx.dma("gpsimd", [(self.dbg[14:15, :], rr[0][0:1, :]),
                                      (self.dbg[15:16, :], h1[0][0:1, :])], "so",
                           [("ve", m_y1)])

                st2 = self.ln_stats(cx, h1, sq, False, [("ve", m_y1)])
                m_bc2 = self.bcast_rows(cx, [(self.row(2), 4, bc[0]), (self.row(3), 5, bc[1])],
                                        [("ve", st2["ms"])])
                def xc2_body(e):
                    last = None
                    for dt in range(DT):
                        e.tensor_tensor(sq[dt], h1[dt], bc[0], ALU.mult)
                        e.tensor_tensor(sq[dt], sq[dt], bc[1], ALU.subtract)
                        last = e.tensor_scalar(xa[dt], sq[dt],
                                               self.pv["ln1_g"][dt][:, 0:1],
                                               self.pv["ln1_b"][dt][:, 0:1],
                                               ALU.mult, ALU.add)
                    return last
                m_xc2 = cx.step("vector", "ve", [("xl", m_bc2)], xc2_body,
                                barrier=False)
                def mix2_body(e):
                    last = None
                    for dt in range(DT):
                        e.tensor_copy(xs_t[:, 1:COL], xa[dt][:, 0:COL - 1])
                        e.tensor_copy(xs_t[:, 0:1], csh2[dt])
                        e.tensor_copy(csh2[dt], xa[dt][:, COL - 1:COL])
                        e.tensor_tensor(d1_t, xa[dt], xs_t, ALU.subtract)
                        for name, dstl in (("cm_k", mixck), ("cm_r", mixcr)):
                            last = e.scalar_tensor_tensor(
                                dstl[dt], d1_t, self.pv[name][dt][:, 0:1], xs_t,
                                ALU.mult, ALU.add)
                    return last
                m_mix2 = cx.step("vector", "ve", [("ve", m_xc2)], mix2_body)

                for ft in range(FT):
                    b = wpar % 2; wpar += 1
                    m_w = cx.dma("sync",
                                 [self.slab_pair(wslab[b][:, 0:1024],
                                                 self.wbf["Wck"], ft * 128,
                                                 (ft + 1) * 128)], "wl",
                                 [("xl", self.m_cast_attn), ("pe", wfree[b]),
                                  ("so", so0)])
                    cb = bank[0 + ft % 2]
                    def ck_body(e, ft=ft, b=b, cb=cb):
                        last = None
                        for j in range(DT):
                            last = e.matmul(cb, wslab[b][:, j * 128:(j + 1) * 128],
                                            mixck[j], start=(j == 0), stop=(j == DT - 1))
                        return last
                    w = [("wl", m_w), ("ve", m_mix2)]
                    if ft >= 2:
                        w.append(("ve", cx.cnt["ve"]))
                        w.append(("se", cx.cnt["se"]))
                    m_ck = cx.step("tensor", "pe", w, ck_body)
                    wfree[b] = m_ck
                    def rl_body(e, ft=ft, cb=cb):
                        return e.activation(out=d1_t, in_=cb, func=AF.Relu)
                    m_rl = cx.step("scalar", "se",
                                   [("pe", m_ck), ("ve", cx.cnt["ve"])], rl_body,
                                   barrier=False)
                    def k2_body(e, ft=ft, cb=cb):
                        return e.tensor_tensor(kk2[ft], cb, d1_t, ALU.mult)
                    cx.step("vector", "ve", [("se", m_rl)], k2_body)
                m_kk2 = cx.cnt["ve"]

                for dt2 in range(DT):
                    b = wpar % 2; wpar += 1
                    m_w = cx.dma("sync",
                                 [self.slab_pair(wslab[b][:, 0:1024],
                                                 self.wbf["Wcr"], dt2 * 128,
                                                 (dt2 + 1) * 128)], "wl",
                                 [("xl", self.m_cast_attn), ("pe", wfree[b]),
                                  ("so", so0)])
                    qb = bank[2 + dt2 % 2]
                    def cr_body(e, dt2=dt2, b=b, qb=qb):
                        last = None
                        for j in range(DT):
                            last = e.matmul(qb, wslab[b][:, j * 128:(j + 1) * 128],
                                            mixcr[j], start=(j == 0), stop=(j == DT - 1))
                        return last
                    w = [("wl", m_w), ("ve", m_mix2)]
                    if dt2 >= 2:
                        w.append(("se", cx.cnt["se"]))
                    m_cr2 = cx.step("tensor", "pe", w, cr_body)
                    wfree[b] = m_cr2
                    def sg_body(e, dt2=dt2, qb=qb):
                        return e.activation(out=rr[dt2], in_=qb, func=AF.Sigmoid)
                    cx.step("scalar", "se", [("pe", m_cr2)], sg_body, barrier=False)
                m_rr = cx.cnt["se"]

                for dt2 in range(DT):
                    bs = []
                    mws = []
                    for half in range(2):
                        b = wpar % 2; wpar += 1
                        mws.append(cx.dma(
                            "sync",
                            [self.slab_pair(wslab[b][:, 0:2048],
                                            self.wbf["Wcv"][half * 2048:(half + 1) * 2048, :],
                                            dt2 * 128, (dt2 + 1) * 128)], "wl",
                            [("xl", self.m_cast_attn), ("pe", wfree[b]),
                             ("so", so0)]))
                        bs.append(b)
                    vb2 = bank[4 + dt2 % 2]
                    def cv_body(e, dt2=dt2, bs=tuple(bs), vb2=vb2):
                        last = None
                        for j in range(FT):
                            sl = wslab[bs[j // 16]]
                            last = e.matmul(vb2, sl[:, (j % 16) * 128:(j % 16 + 1) * 128],
                                            kk2[j], start=(j == 0), stop=(j == FT - 1))
                        return last
                    w = [("wl", mws[1]), ("ve", m_kk2), ("se", m_rr)]
                    if dt2 >= 2:
                        w.append(("ve", cx.cnt["ve"]))
                    m_cv = cx.step("tensor", "pe", w, cv_body)
                    wfree[bs[0]] = m_cv
                    wfree[bs[1]] = m_cv
                    def fin_body(e, dt2=dt2, vb2=vb2):
                        e.scalar_tensor_tensor(d1_t, vb2, 1.0, rr[dt2], ALU.mult, ALU.mult)
                        e.drain(fusable=False)
                        e.tensor_tensor(d1_t, d1_t, h1[dt2], ALU.add)
                        return e.tensor_tensor(xc[dt2], xc[dt2], d1_t, ALU.add)
                    cx.step("vector", "ve", [("pe", m_cv)], fin_body)
                m_fin = cx.cnt["ve"]
                if col == 0 and getattr(self, "debug", False):
                    cx.dma("gpsimd", [(self.dbg[16:17, :], xa[0][0:1, :]),
                                      (self.dbg[17:18, :], kk2[16][0:1, :]),
                                      (self.dbg[18:19, :], rr[0][0:1, :]),
                                      (self.dbg[19:20, :], xc[0][0:1, :])], "so",
                           [("ve", m_fin)])
                if last_phase:
                    self.transpose_out(cx, col, xc, stage, [("ve", m_fin)])
                else:
                    self.store_col(cx, col, xc, m_fin)
                self.emit_cast(cx, 2)
                if col >= 2:
                    self.emit_moet(cx, 32 if col in (2, NCOL - 2) else 64)
            cx.flush()

    def phase_attn_b(self, cx, first_phase, last_phase):
        nc = self.nc
        so0 = cx.cnt["so"]
        with contextlib.ExitStack() as es:
            def sb(name, shape, dtype):
                return es.enter_context(nc.sbuf_tensor(name, shape, dtype))[:, :]
            xc = [sb(f"b_xc{d}", [128, COL], F32) for d in range(DT)]
            sqt = sb("b_sq", [128, DT * COL], F32)
            sq = [sqt[:, d * COL:(d + 1) * COL] for d in range(DT)]
            stage = [sqt[:, s * 1024:(s + 1) * 1024] for s in range(4)]
            bc = [sb(f"b_bc{i}", [128, COL], F32) for i in range(4)]
            rrep = sb("b_rrep", [64, COL], F32)
            negmh = sb("b_negmh", [128, HEADS], F32)
            m4 = sb("b_m4", [128, 16], F32)
            hbf = [sb(f"b_hb{d}", [128, COL], BF16) for d in range(DT)]
            xnbf = [sb(f"b_xn{d}", [128, COL], BF16) for d in range(DT)]
            qh = [sb(f"b_q{h}", [64, COL], BF16) for h in range(HEADS)]
            Kc = sb("b_Kc", [64, T], BF16)
            Vtok = sb("b_Vt", [128, 32 * (DH + 1)], BF16)
            vt_bf = sb("b_vtb", [64, COL], BF16)
            ao = [sb(f"b_ao{d}", [128, COL], BF16) for d in range(DT)]
            Pt = [sb(f"b_P{i}", [128, 4 * COL], BF16) for i in range(2)]
            mask01 = sb("b_msk", [128, 4 * COL], BF16)
            wbuf = sb("b_wbuf", [128, 2 * 1024], BF16)
            wslab = [wbuf[:, 0:1024], wbuf[:, 1024:2048]]
            wkv_sb = sb("b_wkv", [128, DT * 128], BF16)
            bank = self.bank
            bankb = self.bankb
            cx.begin()

            m_msk = cx.dma("sync", [(mask01[:, :], self.c_mask01[...])], "wl",
                           [("so", so0)])
            m_wkvw = cx.dma("sync",
                            [self.slab_pair(wkv_sb[:, :], self.wbf["Wkv"], 0, 128)],
                            "wl", [("xl", self.m_cast_attn), ("so", so0)])

            wpar = 0
            wfree = {0: 0, 1: 0}
            for col in range(NCOL):
                ser = [("so", cx.cnt["so"])]
                if first_phase:
                    m_x = self.transpose_in(cx, col, xc, stage, ser)
                    waits_x = [("ve", m_x)]
                else:
                    m_x = self.load_col(cx, col, xc, ser)
                    waits_x = [("xl", m_x)]
                st = self.ln_stats(cx, xc, sq, True, waits_x)
                m_bc = self.bcast_rows(cx,
                                       [(self.row(2), 0, bc[0]), (self.row(3), 1, bc[1]),
                                        (self.row(4), 2, bc[2]), (self.row(5), 3, bc[3])],
                                       [("ve", st["ms"])])
                def hx_body(e):
                    last = None
                    for dt in range(DT):
                        e.tensor_tensor(sq[dt], xc[dt], bc[0], ALU.mult)
                        e.tensor_tensor(hbf[dt], sq[dt], bc[1], ALU.subtract)
                        e.tensor_tensor(sq[dt], xc[dt], bc[2], ALU.mult)
                        e.tensor_tensor(sq[dt], sq[dt], bc[3], ALU.subtract)
                        last = e.tensor_scalar_mul(xnbf[dt], sq[dt],
                                                   self.pv["gamma"][dt][:, 0:1])
                    return last
                m_h = cx.step("vector", "ve", [("xl", m_bc)], hx_body)
                if col == 0 and getattr(self, "debug", False):
                    cx.dma("gpsimd", [(self.dbg[0:1, :], hbf[0][0:1, :]),
                                      (self.dbg[1:2, :], xnbf[0][0:1, :])], "so",
                           [("ve", m_h)])

                for dt2 in range(DT):
                    b = wpar % 2; wpar += 1
                    m_w = cx.dma("sync",
                                 [self.slab_pair(wslab[b][:, 0:1024],
                                                 self.wbf["Wq"], dt2 * 128,
                                                 (dt2 + 1) * 128)], "wl",
                                 [("xl", self.m_cast_attn), ("pe", wfree[b]),
                                  ("so", so0)])
                    qb = bank[0 + dt2 % 2]
                    def q_body(e, dt2=dt2, b=b, qb=qb):
                        last = None
                        for j in range(DT):
                            last = e.matmul(qb, wslab[b][:, j * 128:(j + 1) * 128],
                                            xnbf[j], start=(j == 0), stop=(j == DT - 1))
                        return last
                    w = [("wl", m_w), ("ve", m_h)]
                    if dt2 >= 2:
                        w.append(("se", cx.cnt["se"]))
                    m_q = cx.step("tensor", "pe", w, q_body)
                    wfree[b] = m_q
                    def qe_body(e, dt2=dt2, qb=qb):
                        e.activation(out=qh[2 * dt2][:, :], in_=qb[0:64, :],
                                     func=AF.Copy, scale=SCALE_B)
                        return e.activation(out=qh[2 * dt2 + 1][:, :], in_=qb[64:128, :],
                                            func=AF.Copy, scale=SCALE_B)
                    cx.step("scalar", "se", [("pe", m_q)], qe_body)
                m_qdone = cx.cnt["se"]
                def kv_body(e):
                    last = None
                    for j in range(DT):
                        last = e.matmul(bank[6], wkv_sb[:, j * 128:(j + 1) * 128],
                                        hbf[j], start=(j == 0), stop=(j == DT - 1))
                    return last
                m_kv = cx.step("tensor", "pe",
                               [("wl", m_wkvw), ("ve", m_h), ("se", m_qdone)], kv_body)
                def kc_body(e, col=col):
                    if col == 0 and getattr(self, "debug", False):
                        e.tensor_copy(ptmp[0], bank[6])
                    e.tensor_copy(Kc[:, col * COL:(col + 1) * COL], bank[6][0:64, :])
                    return e.tensor_copy(vt_bf[:, :], bank[6][64:128, :])
                m_kc = cx.step("vector", "ve", [("pe", m_kv)], kc_body)
                mk_vt = []
                for i in range(4):
                    def tb(e, i=i):
                        return e.transpose(bankb[:, i * 64:(i + 1) * 64],
                                           vt_bf[:, i * 128:(i + 1) * 128],
                                           self.idb_sb[0:64, 0:64])
                    w = [("ve", m_kc)]
                    mk_vt.append(cx.step("tensor", "pe", w, tb))
                def vt_ev(e, col=col):
                    last = None
                    for i in range(4):
                        kb = col * 4 + i
                        e.tensor_copy(Vtok[:, kb * 65:kb * 65 + 64],
                                      bankb[:, i * 64:(i + 1) * 64])
                        last = e.memset(Vtok[:, kb * 65 + 64:kb * 65 + 65], 1.0)
                    return last
                m_vt = cx.step("vector", "ve", [("pe", mk_vt[-1])], vt_ev)
                if col == 0 and getattr(self, "debug", False):
                    cx.dma("gpsimd", [(self.dbg[2:3, :], Kc[0:1, 0:COL]),
                                      (self.dbg[3:4, :], qh[0][0:1, :]),
                                      (self.dbg[4:5, 0:130], Vtok[0:1, 0:130]),
                                      (self.dbg[13:14, :], ptmp[0][0:1, :]),
                                      (self.dbg[14:15, 0:128], self.wbf["Wkv"][0:1, :]),
                                      (self.dbg[15:16, 0:128], wkv_sb[0:1, 0:128])], "so",
                           [("ve", m_vt), ("se", m_qdone)])

                nkc = 4 * (col + 1)
                nkb = col + 1
                nkbp = (nkb + 1) // 2
                m_negmh = {}
                for h in range(HEADS):
                    for qs in range(4):
                        red_marks = []
                        for kbp in range(nkbp):
                            wdt = 1024 if 2 * kbp + 1 < nkb else 512
                            def pa_body(e, h=h, qs=qs, kbp=kbp, wdt=wdt):
                                last = None
                                base = (kbp % 2) * 1024
                                for half in range(wdt // 512):
                                    kb = 2 * kbp + half
                                    last = e.matmul(
                                        self.psA[:, base + half * 512:
                                                 base + (half + 1) * 512],
                                        qh[h][:, qs * 128:(qs + 1) * 128],
                                        Kc[:, kb * COL:(kb + 1) * COL],
                                        start=True, stop=True)
                                return last
                            w = [("se", m_qdone), ("ve", m_vt)]
                            if kbp >= 2:
                                w.append(("ve", red_marks[kbp - 2]))
                            else:
                                w.append(("ve", cx.cnt["ve"]))
                            m_pa = cx.step("tensor", "pe", w, pa_body)
                            def pa_red(e, kbp=kbp, wdt=wdt):
                                base = (kbp % 2) * 1024
                                return e.tensor_reduce(m4[:, kbp:kbp + 1],
                                                       self.psA[:, base:base + wdt],
                                                       AX.X, ALU.max)
                            red_marks.append(cx.step("vector", "ve",
                                                     [("pe", m_pa)], pa_red,
                                                     barrier=False))
                        def pa_fin(e, qs=qs, nkbp=nkbp):
                            e.drain(fusable=False)
                            return e.tensor_reduce(m4[:, 8 + qs:9 + qs],
                                                   m4[:, 0:nkbp], AX.X, ALU.max)
                        cx.step("vector", "ve", [("ve", red_marks[-1])], pa_fin)
                    def mq_body(e, h=h):
                        e.drain(fusable=False)
                        return e.tensor_reduce(m4[:, 12:13], m4[:, 8:12], AX.X,
                                               ALU.max)
                    m_mq = cx.step("vector", "ve", [], mq_body)
                    def mt_body(e, h=h):
                        return e.transpose(bank[5][0:1, 0:128], m4[:, 12:13],
                                           self.idf_sb[:, :])
                    m_mt = cx.step("tensor", "pe", [("ve", m_mq)], mt_body)
                    def mh_body(e, h=h):
                        e.tensor_reduce(m4[:, 13:14][0:1, :], bank[5][0:1, 0:128],
                                        AX.X, ALU.max)
                        e.drain(fusable=False)
                        return e.tensor_scalar_mul(m4[:, 14:15][0:1, :],
                                                   m4[:, 13:14][0:1, :], -1.0)
                    m_mh = cx.step("vector", "ve", [("pe", m_mt)], mh_body)
                    m_negmh[h] = self.bcast_rows_n(
                        cx, [(m4[0:1, 14:15], 24 + h, negmh[:, h:h + 1], 1)],
                        [("ve", m_mh)])

                prev_aw = 0
                for h in range(HEADS):
                    av = bank[4]
                    nq4 = (nkc + 3) // 4
                    exp_marks = {}
                    av_marks = {}
                    for q4 in range(nq4):
                        kcs = [kc for kc in range(4 * q4, min(4 * q4 + 4, nkc))]
                        def qk_body(e, h=h, kcs=tuple(kcs)):
                            last = None
                            for kc in kcs:
                                last = e.matmul(
                                    self.psA[:, (kc % 4) * 512:(kc % 4 + 1) * 512],
                                    Kc[:, kc * 128:(kc + 1) * 128],
                                    qh[h][:, :], start=True, stop=True)
                            return last
                        w = [("xl", m_negmh[h])]
                        if q4 >= 1:
                            w.append(("se", exp_marks[q4 - 1]))
                        elif h > 0:
                            w.append(("se", cx.cnt["se"]))
                        m_qk = cx.step("tensor", "pe", w, qk_body)
                        wdt = 512 * len(kcs)
                        def exp_body(e, q4=q4, wdt=wdt, h=h):
                            return e.activation(out=Pt[q4 % 2][:, 0:wdt],
                                                in_=self.psA[:, 0:wdt],
                                                func=AF.Exp,
                                                bias=negmh[:, h:h + 1])
                        w = [("ve", cx.cnt["ve"]), ("pe", m_qk)]
                        if q4 >= 2:
                            w.append(("pe", av_marks[q4 - 2]))
                        exp_marks[q4] = cx.step("scalar", "se", w, exp_body)
                        msk_marks = []
                        for kc in kcs:
                            if kc >= 4 * col:
                                mi = kc - 4 * col
                                def msk_body(e, q4=q4, kc=kc, mi=mi):
                                    sl = Pt[q4 % 2][:, (kc % 4) * 512:
                                                    (kc % 4 + 1) * 512]
                                    return e.tensor_tensor(
                                        sl, sl, mask01[:, mi * COL:(mi + 1) * COL],
                                        ALU.mult)
                                msk_marks.append(cx.step(
                                    "vector", "ve", [("se", exp_marks[q4])],
                                    msk_body))
                        av_w = [("se", exp_marks[q4])]
                        if msk_marks:
                            av_w.append(("ve", msk_marks[-1]))
                        if q4 == 0:
                            av_w.append(("ve", prev_aw))
                        def av_body(e, h=h, q4=q4, kcs=tuple(kcs), nkc=nkc):
                            last = None
                            for kc in kcs:
                                last = e.matmul(
                                    av[0:65, :], Vtok[:, kc * 65:kc * 65 + 65],
                                    Pt[q4 % 2][:, (kc % 4) * 512:(kc % 4) * 512 + 512],
                                    start=(kc == 0), stop=(kc == nkc - 1))
                            return last
                        av_marks[q4] = cx.step("tensor", "pe", av_w, av_body)
                    def rec_body(e, h=h):
                        return e.reciprocal(self.rown(7, COL), av[64:65, :])
                    m_rc = cx.step("vector", "ve", [("pe", av_marks[nq4 - 1])],
                                   rec_body)
                    def rep_body(e, h=h):
                        return e.matmul(bank[6][0:64, :], self.onesrow[0:1, 0:64],
                                        self.rown(7, COL), start=True, stop=True)
                    m_rep = cx.step("tensor", "pe",
                                    [("ve", m_rc), ("ve", self.m_onesrow)], rep_body)
                    def rcp_body(e, h=h):
                        return e.tensor_copy(rrep[:, :], bank[6][0:64, :])
                    m_rcp = cx.step("vector", "ve", [("pe", m_rep)], rcp_body)
                    def aw_body(e, h=h):
                        return e.tensor_tensor(ao[h // 2][(h % 2) * 64:(h % 2) * 64 + 64, :],
                                               av[0:65, :][0:64, :], rrep[:, :],
                                               ALU.mult)
                    prev_aw = cx.step("vector", "ve", [("ve", m_rcp)], aw_body)
                m_att = cx.cnt["ve"]

                for dt2 in range(DT):
                    b = wpar % 2; wpar += 1
                    m_w = cx.dma("sync",
                                 [self.slab_pair(wslab[b][:, 0:1024],
                                                 self.wbf["Wob"], dt2 * 128,
                                                 (dt2 + 1) * 128)], "wl",
                                 [("pe", wfree[b]), ("so", so0)])
                    ob = bank[0 + dt2 % 2]
                    def wo_body(e, dt2=dt2, b=b, ob=ob):
                        last = None
                        for j in range(DT):
                            last = e.matmul(ob, wslab[b][:, j * 128:(j + 1) * 128],
                                            ao[j], start=(j == 0), stop=(j == DT - 1))
                        return last
                    w = [("wl", m_w), ("ve", m_att)]
                    if dt2 >= 2:
                        w.append(("ve", cx.cnt["ve"]))
                    m_wo = cx.step("tensor", "pe", w, wo_body)
                    wfree[b] = m_wo
                    def res_body(e, dt2=dt2, ob=ob):
                        return e.tensor_tensor(xc[dt2], xc[dt2], ob, ALU.add)
                    cx.step("vector", "ve", [("pe", m_wo)], res_body)
                m_fin = cx.cnt["ve"]
                if last_phase:
                    self.transpose_out(cx, col, xc, stage, [("ve", m_fin)])
                else:
                    self.store_col(cx, col, xc, m_fin)
            cx.flush()

    def build(self):
        nc = bass.Bass("TRN2", target_bir_lowering=False)
        self.nc = nc
        self.declare(nc)
        es = contextlib.ExitStack()
        with es:
            blk = es.enter_context(nc.Block())
            cx = Cx(nc, es, blk)
            self.alloc_fixed(nc, es)
            self.preamble(cx, es)
            first = True
            for ph in self.phases:
                last = ph == self.phases[-1]
                if ph in ("f1", "f2"):
                    self.phase_ffn(cx, first, last)
                elif ph == "a":
                    self.phase_attn_a(cx, first, last)
                else:
                    self.phase_attn_b(cx, first, last)
                first = False
            total_so = cx.cnt["so"]
            cx.begin()
            def fin(q):
                q.wait_ge(cx.sem["so"], total_so)
            cx.emit("gpsimd", fin)
            cx.flush()
        return nc


def make_in_maps(inputs):
    x = np.ascontiguousarray(inputs["L0_attn_a_x"])
    base = {
        "ln0_g": inputs["L0_attn_a_ln0_g"], "ln0_b": inputs["L0_attn_a_ln0_b"],
        "ln1_g": inputs["L0_attn_a_ln1_g"], "ln1_b": inputs["L0_attn_a_ln1_b"],
        "time_decay": inputs["L0_attn_a_time_decay"],
        "time_first": inputs["L0_attn_a_time_first"],
        "tm_k": inputs["L0_attn_a_tm_k"], "tm_v": inputs["L0_attn_a_tm_v"],
        "tm_r": inputs["L0_attn_a_tm_r"],
        "Wk": inputs["L0_attn_a_Wk"], "Wv": inputs["L0_attn_a_Wv"],
        "Wr": inputs["L0_attn_a_Wr"], "Wo": inputs["L0_attn_a_Wo"],
        "cm_k": inputs["L0_attn_a_cm_k"], "cm_r": inputs["L0_attn_a_cm_r"],
        "Wck": inputs["L0_attn_a_Wck"], "Wcv": inputs["L0_attn_a_Wcv"],
        "Wcr": inputs["L0_attn_a_Wcr"],
        "gamma": inputs["L0_attn_b_gamma"], "Wq": inputs["L0_attn_b_Wq"],
        "Wkv": inputs["L0_attn_b_Wkv"], "Wob": inputs["L0_attn_b_Wo"],
        "router_w": inputs["L0_ffn_router_w"],
        "w_gate": inputs["L0_ffn_w_gate"], "w_up": inputs["L0_ffn_w_up"],
        "w_down": inputs["L0_ffn_w_down"],
    }
    base = {k: np.ascontiguousarray(v) for k, v in base.items()}
    in_maps = []
    for c in range(NCORES):
        m = dict(base)
        m["x"] = x[c]
        in_maps.append(m)
    return in_maps


def kernel(**inputs):
    nc = Build().build()
    in_maps = make_in_maps(inputs)
    res = run_bass_kernel_spmd(nc, in_maps, core_ids=list(range(NCORES)))
    return np.stack([res.results[c]["y"] for c in range(NCORES)], axis=0)


# version 14 (final)
# speedup vs baseline: 1.2347x; 1.2347x over previous
import contextlib
import numpy as np
import ml_dtypes
import concourse.bass as bass
import concourse.mybir as mybir
from concourse.bass_utils import run_bass_kernel_spmd

F32 = mybir.dt.float32
BF16 = mybir.dt.bfloat16
AF = mybir.ActivationFunctionType
ALU = mybir.AluOpType
AX = mybir.AxisListType

NCORES = 8
T = 4096
D = 1024
FF = 4096
HID = 2048
E = 8
HEADS = 16
DH = 64
SCALE_B = DH ** -0.5
COL = 512
NCOL = T // COL
DT = D // 128
HT = HID // 128
FT = FF // 128
EPS = 1e-5
PINIT = -10000.0

ENGS = ("sync", "gpsimd", "tensor", "vector", "scalar")


class Cx:

    def __init__(self, nc, es, blk):
        self.nc = nc
        self.es = es
        self.blk = blk
        self.sem = {}
        self.cnt = {}
        for name in ("wl", "xl", "so", "pe", "ve", "se", "mo", "mt"):
            self.sem[name] = es.enter_context(nc.semaphore(name))
            self.cnt[name] = 0
        self._streams = None

    def begin(self):
        self._streams = {e: [] for e in ENGS}

    def emit(self, eng, fn):
        self._streams[eng].append(fn)

    def step(self, eng, sem, waits, body, barrier=None):
        w = tuple((n, v) for n, v in waits)
        s = self.sem
        if barrier is None:
            barrier = eng in ("vector", "scalar", "gpsimd")
        def go(e):
            for nm, v in w:
                e.wait_ge(s[nm], v)
            last = body(e)
            if barrier:
                e.drain(fusable=False).then_inc(s[sem], 1)
            else:
                last.then_inc(s[sem], 1)
        self.cnt[sem] += 1
        self.emit(eng, go)
        return self.cnt[sem]

    def dma(self, eng, pairs, sem, waits=()):
        w = tuple(waits)
        s = self.sem
        def go(q):
            for nm, v in w:
                q.wait_ge(s[nm], v)
            for out, in_ in pairs:
                q.dma_start(out=out, in_=in_).then_inc(s[sem], 16)
        self.cnt[sem] += 16 * len(pairs)
        self.emit(eng, go)
        return self.cnt[sem]

    def dma_tr(self, eng, pairs, sem, waits=()):
        w = tuple(waits)
        s = self.sem
        def go(q):
            for nm, v in w:
                q.wait_ge(s[nm], v)
            for out, in_ in pairs:
                q.dma_start_transpose(out, in_).then_inc(s[sem], 16)
        self.cnt[sem] += 16 * len(pairs)
        self.emit(eng, go)
        return self.cnt[sem]

    def flush(self):
        for eng in ENGS:
            closures = self._streams[eng]
            if not closures:
                continue
            def body(e, closures=closures):
                for fn in closures:
                    fn(e)
            getattr(self.blk, eng)(body)
        self._streams = None


class Build:
    def __init__(self, phases=("a", "f1", "b", "f2")):
        self.phases = phases

    def declare(self, nc):
        self.x_in = nc.dram_tensor("x", [T, D], F32, kind="ExternalInput").ap()
        self.w = {}
        def win(name, shape):
            self.w[name] = nc.dram_tensor(name, list(shape), F32, kind="ExternalInput").ap()
        for n in ("ln0_g", "ln0_b", "ln1_g", "ln1_b", "time_decay", "time_first",
                  "tm_k", "tm_v", "tm_r", "cm_k", "cm_r", "gamma"):
            win(n, (D,))
        win("Wk", (D, D)); win("Wv", (D, D)); win("Wr", (D, D)); win("Wo", (D, D))
        win("Wck", (D, FF)); win("Wcv", (FF, D)); win("Wcr", (D, D))
        win("Wq", (D, D)); win("Wkv", (D, 2 * DH)); win("Wob", (D, D))
        win("router_w", (E, D))
        win("w_gate", (E, HID, D)); win("w_up", (E, HID, D)); win("w_down", (E, D, HID))

        self.y_out = nc.dram_tensor("y", [T, D], F32, kind="ExternalOutput").ap()
        self.xt_s = nc.dram_tensor("xt_s", [D, T], F32, kind="Internal").ap()
        self.wbf = {}
        for n, shape in (("Wk", (D, D)), ("Wv", (D, D)), ("Wr", (D, D)),
                         ("Wo", (D, D)), ("Wck", (D, FF)), ("Wcv", (FF, D)),
                         ("Wcr", (D, D)), ("Wq", (D, D)), ("Wkv", (D, 2 * DH)),
                         ("Wob", (D, D)), ("w_gate", (E, HID, D)),
                         ("w_up", (E, HID, D)), ("w_down", (E, D, HID))):
            self.wbf[n] = nc.dram_tensor(n + "_bf", list(shape), BF16, kind="Internal").ap()
        self.wguT = nc.dram_tensor("wguT", [E, D, 2 * HID], BF16, kind="Internal").ap()
        self.wdT = nc.dram_tensor("wdT", [E, 128, HT, D], BF16, kind="Internal").ap()
        self.rows_s = nc.dram_tensor("rows_s", [64, COL], F32, kind="Internal").ap()
        self.dbg = nc.dram_tensor("dbg", [64, COL], F32, kind="ExternalOutput").ap()

        self.c_ones = nc.inline_tensor(np.ones((128, 1), np.float32), name="c_ones").ap()
        self.c_idb = nc.inline_tensor(np.eye(128, dtype=ml_dtypes.bfloat16), name="c_idb").ap()
        self.c_idf = nc.inline_tensor(np.eye(128, dtype=np.float32), name="c_idf").ap()
        m01 = np.zeros((4, 128, COL), np.float32)
        for kc in range(4):
            j = np.arange(128)[:, None] + kc * 128
            q = np.arange(COL)[None, :]
            m01[kc] = (j <= q).astype(np.float32)
        self.c_mask01 = nc.inline_tensor(
            np.ascontiguousarray(m01.transpose(1, 0, 2).reshape(128, 4 * COL))
            .astype(ml_dtypes.bfloat16), name="c_mask01").ap()
        self.const_vals = [EPS, EPS * EPS, -0.5, 1.0 + EPS, -1.0, SCALE_B, 1.0, 0.0,
                           1.0 / D, -1e30, 0.5, 2.0]
        self.c_scal = nc.inline_tensor(
            np.array([self.const_vals], np.float32), name="c_scal").ap()

    def alloc_fixed(self, nc, es):
        def sb(name, shape, dtype):
            return es.enter_context(nc.sbuf_tensor(name, shape, dtype))[:, :]
        def psb(name, shape, dtype=F32):
            return es.enter_context(nc.psum_tensor(name, shape, dtype))[:, :]
        self.rowbuf = sb("rowbuf", [1, 9 * COL], F32)
        self.ones_sb = sb("ones_sb", [128, 1], F32)
        self.onesrow = sb("onesrow", [1, 128], F32)
        self.idb_sb = sb("idb_sb", [128, 128], BF16)
        self.idf_sb = sb("idf_sb", [128, 128], F32)
        self.cvec = sb("cvec", [128, 12 * DT], F32)
        self.negS = sb("negS", [1, E], F32)
        self.routerT = sb("routerT", [128, DT * E], F32)
        self.combT = sb("combT", [E, COL], F32)
        self.comb = sb("comb", [128, 4 * E], F32)
        self.stv = sb("stv", [128, 48], F32)
        self.const_sb = sb("const_sb", [128, len(self.const_vals)], F32)
        self.psA = psb("psA", [128, 7 * 512], F32)
        self.bank = [self.psA[:, i * 512:(i + 1) * 512] for i in range(7)]
        self.bankb = psb("bankb", [128, 1024], BF16)

    def row(self, i):
        return self.rowbuf[0:1, i * COL:(i + 1) * COL]

    def rown(self, i, n):
        return self.rowbuf[0:1, i * COL:i * COL + n]

    def bcast_rows_n(self, cx, items, waits, eng="gpsimd", sem="xl"):
        out_pairs = [(self.rows_s[slot:slot + 1, 0:n], srcap)
                     for srcap, slot, _, n in items]
        m = cx.dma(eng, out_pairs, sem, waits)
        in_pairs = [(dst, self.rows_s[slot:slot + 1, 0:n].partition_broadcast(128))
                    for _, slot, dst, n in items]
        return cx.dma(eng, in_pairs, sem, [(sem, m)])

    def bcast_rows(self, cx, items, waits, eng="gpsimd", sem="xl"):
        out_pairs = [(self.rows_s[slot:slot + 1, :], row) for row, slot, _ in items]
        m = cx.dma(eng, out_pairs, sem, waits)
        in_pairs = [(bc, self.rows_s[slot:slot + 1, :].partition_broadcast(128))
                    for _, slot, bc in items]
        return cx.dma(eng, in_pairs, sem, [(sem, m)])

    def ln_stats(self, cx, src, sq, fused_s12, waits):
        bk0, bk1 = self.bank[5], self.bank[6]
        def sq_body(e):
            last = None
            for dt in range(DT):
                last = e.activation(out=sq[dt], in_=src[dt], func=AF.Square)
            return last
        m_sq = cx.step("scalar", "se", waits, sq_body)
        def sum_body(e):
            for dt in range(DT):
                e.matmul(bk0[0:1, :], self.ones_sb[:, :], src[dt],
                         start=(dt == 0), stop=(dt == DT - 1))
            last = None
            for dt in range(DT):
                last = e.matmul(bk1[0:1, :], self.ones_sb[:, :], sq[dt],
                                start=(dt == 0), stop=(dt == DT - 1))
            return last
        m_sum = cx.step("tensor", "pe", list(waits) + [("se", m_sq), ("wl", self.m_fixed)], sum_body)
        def mv_body(e):
            e.tensor_scalar_mul(self.row(0), bk0[0:1, :], 1.0 / D)
            e.tensor_scalar_mul(self.row(1), bk1[0:1, :], 1.0 / D)
            e.tensor_tensor(self.row(6), self.row(0), self.row(0), ALU.mult)
            return e.tensor_tensor(self.row(1), self.row(1), self.row(6), ALU.subtract)
        m_mv = cx.step("vector", "ve", [("pe", m_sum)], mv_body,
                       barrier=False)
        def rstd_body(e):
            e.activation(out=self.row(6), in_=self.row(1), func=AF.Ln, bias=EPS)
            if fused_s12:
                e.activation(out=self.row(8), in_=self.row(1), func=AF.Ln,
                             bias=EPS * EPS, scale=1.0 + EPS)
            last = e.activation(out=self.row(2), in_=self.row(6), func=AF.Exp, scale=-0.5)
            if fused_s12:
                last = e.activation(out=self.row(4), in_=self.row(8), func=AF.Exp,
                                    scale=-0.5)
            return last
        m_rstd = cx.step("scalar", "se", [("ve", m_mv)], rstd_body,
                         barrier=False)
        def ms_body(e):
            last = e.tensor_tensor(self.row(3), self.row(0), self.row(2), ALU.mult)
            if fused_s12:
                last = e.tensor_tensor(self.row(5), self.row(0), self.row(4), ALU.mult)
            return last
        m_ms = cx.step("vector", "ve", [("se", m_rstd)], ms_body)
        return {"rstd": m_rstd, "ms": m_ms, "sumdone": m_sum}

    def transpose_in(self, cx, col, dst, stage, waits, eng="gpsimd", sem="xl"):
        c0 = col * COL
        pairs = [(stage[s], self.x_in[c0 + s * 128:c0 + (s + 1) * 128, :])
                 for s in range(4)]
        m_ld = cx.dma(eng, pairs, sem, waits)
        marks = []
        ve0 = self.cnt_ve(cx)
        for s in range(4):
            def tb(e, s=s):
                last = None
                for dt in range(DT):
                    last = e.transpose(
                        self.bank[3 + dt % 2][:, (dt // 2) * 128:(dt // 2) * 128 + 128],
                        stage[s][:, dt * 128:(dt + 1) * 128], self.idf_sb[:, :])
                return last
            w = [(sem, m_ld), ("wl", self.m_fixed)]
            if s >= 1:
                w.append(("ve", ve0 + s))
            marks.append(cx.step("tensor", "pe", w, tb))
        for s in range(4):
            def vb(e, s=s):
                last = None
                for dt in range(DT):
                    last = e.tensor_copy(
                        dst[dt][:, s * 128:(s + 1) * 128],
                        self.bank[3 + dt % 2][:, (dt // 2) * 128:(dt // 2) * 128 + 128])
                return last
            cx.step("vector", "ve", [("pe", marks[s])], vb)
        return cx.cnt["ve"]

    def cnt_ve(self, cx):
        return cx.cnt["ve"]

    def transpose_out(self, cx, col, src, stage, waits):
        c0 = col * COL
        marks = []
        ve0 = cx.cnt["ve"]
        for s in range(4):
            def tb(e, s=s):
                last = None
                for dt in range(DT):
                    last = e.transpose(
                        self.bank[3 + dt % 2][:, (dt // 2) * 128:(dt // 2) * 128 + 128],
                        src[dt][:, s * 128:(s + 1) * 128], self.idf_sb[:, :])
                return last
            w = list(waits) + [("wl", self.m_fixed)]
            if s >= 1:
                w.append(("ve", ve0 + s))
            marks.append(cx.step("tensor", "pe", w, tb))
        for s in range(4):
            def vb(e, s=s):
                last = None
                for dt in range(DT):
                    last = e.tensor_copy(
                        stage[s][:, dt * 128:(dt + 1) * 128],
                        self.bank[3 + dt % 2][:, (dt // 2) * 128:(dt // 2) * 128 + 128])
                return last
            mv = cx.step("vector", "ve", [("pe", marks[s])], vb)
            cx.dma("gpsimd",
                   [(self.y_out[c0 + s * 128:c0 + (s + 1) * 128, :], stage[s])],
                   "so", [("ve", mv)])
        return cx.cnt["so"]

    def load_col(self, cx, col, dst, waits, eng="gpsimd", sem="xl"):
        pairs = [(dst[dt], self.xt_s[dt * 128:(dt + 1) * 128, col * COL:(col + 1) * COL])
                 for dt in range(DT)]
        return cx.dma(eng, pairs, sem, waits)

    def store_col(self, cx, col, src, wait_ve, eng="gpsimd"):
        pairs = [(self.xt_s[dt * 128:(dt + 1) * 128, col * COL:(col + 1) * COL], src[dt])
                 for dt in range(DT)]
        return cx.dma(eng, pairs, "so", [("ve", wait_ve)])

    def slab_pair(self, buf, wsrc, c0, c1):
        nj = wsrc.shape[0] // 128
        cw = c1 - c0
        src3 = wsrc[:, c0:c1].rearrange("(j p) c -> p j c", p=128)
        dst3 = buf[:, 0:nj * cw].rearrange("p (j c) -> p j c", c=cw)
        return (dst3, src3)

    def preamble(self, cx, es):
        nc = self.nc
        cx.begin()
        cx.dma("sync", [(self.ones_sb[:, :], self.c_ones[:, :]),
                        (self.idb_sb[:, :], self.c_idb[:, :]),
                        (self.idf_sb[:, :], self.c_idf[:, :]),
                        (self.const_sb[:, :],
                         self.c_scal[0:1, :].partition_broadcast(128))], "wl")
        self.m_fixed = cx.cnt["wl"]
        for i, v in enumerate(self.const_vals):
            nc.const_aps.register(F32, float(v), self.const_sb[:, i:i + 1])
        def ones_init(e):
            return e.memset(self.onesrow[0:1, :], 1.0)
        m_or = cx.step("vector", "ve", [], ones_init)
        self.m_onesrow = m_or
        names = ("ln0_g", "ln0_b", "ln1_g", "ln1_b", "tm_k", "tm_v", "tm_r",
                 "cm_k", "cm_r", "time_decay", "time_first", "gamma")
        self.pv = {}
        pairs = []
        for i, n in enumerate(names):
            cols = []
            for dt in range(DT):
                ap = self.cvec[:, i * DT + dt:i * DT + dt + 1]
                src = self.w[n].rearrange("(a p) -> a p", p=128)[dt:dt + 1, :]
                pairs.append((ap, src.rearrange("a p -> p a")))
                cols.append(ap)
            self.pv[n] = cols
        self.m_cvec = cx.dma("gpsimd", pairs, "xl")
        kvr_w = ("Wk", "Wv", "Wr")
        rest_w = [n for n in self.wbf
                  if not n.startswith("w_") and n not in kvr_w]
        cx.dma("gpsimd", [(self.wbf[n][...], self.w[n][...]) for n in kvr_w], "xl")
        self.m_cast_kvr = cx.cnt["xl"]
        cx.dma("gpsimd", [(self.wbf[n][...], self.w[n][...]) for n in rest_w], "xl")
        self.m_cast_attn = cx.cnt["xl"]
        self.m_cast_moe = {}
        self.cast_jobs = [(n, h) for h in range(2)
                          for n in ("w_gate", "w_up", "w_down")]
        self.cast_state = {"pos": 0}
        router_es = contextlib.ExitStack()
        self.router_sb = router_es.enter_context(
            nc.sbuf_tensor("router_sb", [E, D], F32))[:, :]
        m_r = cx.dma("sync", [(self.router_sb[:, :], self.w["router_w"][...])], "wl")
        mk = []
        ve0 = cx.cnt["ve"]
        for dt in range(DT):
            def tb(e, dt=dt):
                return e.transpose(self.bank[dt % 2][:, 0:E],
                                   self.router_sb[:, dt * 128:(dt + 1) * 128],
                                   self.idf_sb[0:E, 0:E])
            w = [("wl", m_r), ("wl", self.m_fixed)]
            if dt >= 2:
                w.append(("ve", ve0 + dt - 1))
            mk.append(cx.step("tensor", "pe", w, tb))
        for dt in range(DT):
            def vb(e, dt=dt):
                return e.tensor_copy(self.routerT[:, dt * E:(dt + 1) * E],
                                     self.bank[dt % 2][:, 0:E])
            cx.step("vector", "ve", [("pe", mk[dt])], vb)
        m_rT = cx.cnt["ve"]
        def s_body(e):
            last = None
            for dt in range(DT):
                last = e.matmul(self.bank[2][0:1, 0:E], self.ones_sb[:, :],
                                self.routerT[:, dt * E:(dt + 1) * E],
                                start=(dt == 0), stop=(dt == DT - 1))
            return last
        m_s = cx.step("tensor", "pe", [("ve", m_rT)], s_body)
        def nS_body(e):
            return e.tensor_scalar_mul(self.negS[0:1, :], self.bank[2][0:1, 0:E], -1.0)
        cx.step("vector", "ve", [("pe", m_s)], nS_body)

        jobs = []
        for e_i in range(E):
            for n in ("w_gate", "w_up"):
                for dt in range(DT):
                    jobs.append((n, e_i, dt))
            for ht in range(HT):
                jobs.append(("w_down", e_i, ht))
        self.moet_jobs = jobs
        self.moet_state = {"pos": 0, "si": 0, "last": {0: 0, 1: 0}}
        if "a" not in self.phases:
            self.emit_cast(cx, len(self.cast_jobs))
            with nc.sbuf_tensor("stg_moe", [128, 2 * 2048], BF16,
                                side="right") as stg_th:
                self.stg_moe = stg_th[:, :]
                self.emit_moet(cx, len(jobs))
                cx.flush()
        else:
            cx.flush()
        router_es.close()

    def emit_cast(self, cx, k):
        st = self.cast_state
        while k > 0 and st["pos"] < len(self.cast_jobs):
            n, h = self.cast_jobs[st["pos"]]
            st["pos"] += 1
            k -= 1
            cx.dma("gpsimd",
                   [(self.wbf[n][4 * h:4 * h + 4, :, :],
                     self.w[n][4 * h:4 * h + 4, :, :])], "xl")
            self.m_cast_moe[(n, h)] = cx.cnt["xl"]
        if st["pos"] >= len(self.cast_jobs):
            self.m_cast = cx.cnt["xl"]

    def emit_moet(self, cx, n):
        st = self.moet_state
        stg = [self.stg_moe[:, 0:2048], self.stg_moe[:, 2048:4096]]
        while n > 0 and st["pos"] < len(self.moet_jobs):
            name, e_i, i = self.moet_jobs[st["pos"]]
            st["pos"] += 1
            n -= 1
            b = st["si"] % 2
            st["si"] += 1
            w = [("xl", self.m_cast_moe[(name, e_i // 4)])]
            if st["last"][b]:
                w.append(("mo", st["last"][b]))
            if name == "w_down":
                mt = cx.dma_tr("sync",
                               [(stg[b][:, 0:1024],
                                 self.wbf["w_down"][e_i, :, i * 128:(i + 1) * 128])],
                               "mt", w)
                st["last"][b] = cx.dma(
                    "sync", [(self.wdT[e_i, :, i, :], stg[b][:, 0:1024])],
                    "mo", [("mt", mt)])
            else:
                dstf = 0 if name == "w_gate" else 1
                mt = cx.dma_tr("sync",
                               [(stg[b][:, 0:2048],
                                 self.wbf[name][e_i, :, i * 128:(i + 1) * 128])],
                               "mt", w)
                st["last"][b] = cx.dma(
                    "sync",
                    [(self.wguT[e_i, i * 128:(i + 1) * 128,
                                dstf * HID:(dstf + 1) * HID], stg[b][:, 0:2048])],
                    "mo", [("mt", mt)])
        if st["pos"] >= len(self.moet_jobs):
            self.m_moeT = cx.cnt["mo"]

    def phase_ffn(self, cx, first_phase, last_phase):
        nc = self.nc
        self._ffn_n = getattr(self, "_ffn_n", 0) + 1
        tag = f"f{self._ffn_n}"
        so0 = cx.cnt["so"]
        with contextlib.ExitStack() as es:
            def sb(name, shape, dtype):
                return es.enter_context(nc.sbuf_tensor(tag + name, shape, dtype))[:, :]
            xc = [sb(f"_xc{d}", [128, COL], F32) for d in range(DT)]
            sqt = sb("_sq", [128, DT * COL], F32)
            sq = [sqt[:, d * COL:(d + 1) * COL] for d in range(DT)]
            stage = [sqt[:, s * 1024:(s + 1) * 1024] for s in range(4)]
            accb = [[sb(f"_ac{w}_{d}", [128, COL], BF16) for d in range(DT)]
                    for w in range(2)]
            hbf = [[sb(f"_hb{w}_{d}", [128, COL], BF16) for d in range(DT)]
                   for w in range(2)]
            crep = [sb(f"_cr{w}", [128, COL], F32) for w in range(2)]
            bc0 = crep[0]
            bc1 = crep[1]
            small = sb("_small", [128, 64], F32)
            silu = [sb(f"_si{h}", [128, COL], BF16) for h in range(2)]
            hebf = [sb(f"_he{h}", [128, COL], BF16) for h in range(HT)]
            wgu = [sb(f"_wg{d}", [128, 2 * HID], BF16) for d in range(DT)]
            wd = sb("_wd", [128, HT * D], BF16)
            bank = self.bank
            cx.begin()

            prev_gu_pe = 0
            prev_dn_pe = 0
            crep_free = {}
            for win in range(NCOL // 2):
                cols = (2 * win, 2 * win + 1)
                ser = [("so", cx.cnt["so"])]
                m_h = {}
                m_crows = {}
                for ci, col in enumerate(cols):
                    serc = list(ser)
                    if ci == 1:
                        serc += [("pe", cx.cnt["pe"]), ("ve", cx.cnt["ve"])]
                    if first_phase:
                        m_x = self.transpose_in(cx, col, xc, stage, serc)
                        waits_x = [("ve", m_x)]
                    else:
                        m_x = self.load_col(cx, col, xc, serc)
                        waits_x = [("xl", m_x)]
                    st = self.ln_stats(cx, xc, sq, False, waits_x)
                    m_bc = self.bcast_rows(cx, [(self.row(2), 0, bc0),
                                                (self.row(3), 1, bc1)],
                                           [("ve", st["ms"])])
                    def h_body(e, ci=ci):
                        last = None
                        for dt in range(DT):
                            e.tensor_tensor(sq[dt], xc[dt], bc0, ALU.mult)
                            last = e.tensor_tensor(hbf[ci][dt], sq[dt], bc1,
                                                   ALU.subtract)
                        return last
                    m_h[ci] = cx.step("vector", "ve", [("xl", m_bc)], h_body)

                    rstd4 = small[:, 0:4]
                    mk_rc = []
                    for ts in range(4):
                        def tb(e, ts=ts):
                            return e.transpose(bank[6][:, ts:ts + 1],
                                               self.rowbuf[0:1, 2 * COL + ts * 128:
                                                           2 * COL + ts * 128 + 128],
                                               self.idf_sb[0:1, 0:1])
                        w = [("se", st["rstd"])]
                        if ts == 0:
                            w.append(("ve", st["ms"]))
                        mk_rc.append(cx.step("tensor", "pe", w, tb))
                    def rc_ev(e):
                        return e.tensor_copy(rstd4, bank[6][:, 0:4])
                    m_rc = cx.step("vector", "ve", [("pe", mk_rc[-1])], rc_ev)

                    lgt = small[:, 8:16]
                    m1 = small[:, 16:17]; nm1 = small[:, 17:18]
                    t8 = small[:, 24:32]; m2 = small[:, 32:33]
                    gap = small[:, 33:34]; den = small[:, 34:35]
                    rden = small[:, 35:36]
                    num = small[:, 40:48]; msk = small[:, 48:56]
                    for ts in range(4):
                        def lg_body(e, ts=ts):
                            for dt in range(DT):
                                e.matmul(bank[4][:, 0:E],
                                         xc[dt][:, ts * 128:(ts + 1) * 128],
                                         self.routerT[:, dt * E:(dt + 1) * E],
                                         start=(dt == 0), stop=False)
                            return e.matmul(bank[4][:, 0:E],
                                            self.rowbuf[0:1, ts * 128:(ts + 1) * 128],
                                            self.negS[0:1, :], start=False, stop=True)
                        w = [("ve", m_rc)]
                        if ts > 0:
                            w.append(("ve", cx.cnt["ve"]))
                        m_lg = cx.step("tensor", "pe", w, lg_body)
                        def t2a(e, ts=ts):
                            e.drain(fusable=False)
                            e.tensor_scalar_mul(lgt, bank[4][:, 0:E],
                                                rstd4[:, ts:ts + 1])
                            e.drain(fusable=False)
                            e.tensor_reduce(m1, lgt, AX.X, ALU.max)
                            e.drain(fusable=False)
                            e.tensor_scalar_mul(nm1, m1, -1.0)
                            e.tensor_single_scalar(t8, lgt, m1[:, 0:1], ALU.is_ge)
                            e.drain(fusable=False)
                            e.scalar_tensor_tensor(t8, t8, -1e30, lgt, ALU.mult,
                                                   ALU.add)
                            e.drain(fusable=False)
                            e.tensor_reduce(m2, t8, AX.X, ALU.max)
                            e.drain(fusable=False)
                            return e.tensor_single_scalar(msk, lgt, m2[:, 0:1],
                                                          ALU.is_ge)
                        m_a = cx.step("vector", "ve", [("pe", m_lg)], t2a)
                        def t2s(e, ts=ts):
                            e.activation(out=num, in_=lgt, func=AF.Exp,
                                         bias=nm1[:, 0:1])
                            return e.activation(out=gap, in_=m2, func=AF.Exp,
                                                bias=nm1[:, 0:1])
                        m_b = cx.step("scalar", "se", [("ve", m_a)], t2s)
                        def t2c(e, ts=ts):
                            e.tensor_scalar_add(den, gap, 1.0)
                            e.tensor_tensor(num, num, msk, ALU.mult)
                            e.reciprocal(rden, den)
                            e.drain(fusable=False)
                            return e.tensor_scalar_mul(
                                self.comb[:, ts * E:(ts + 1) * E], num,
                                rden[:, 0:1])
                        cx.step("vector", "ve", [("se", m_b)], t2c)
                    mk_ct = []
                    for ts in range(4):
                        def tb(e, ts=ts):
                            return e.transpose(bank[6][0:E, ts * 128:(ts + 1) * 128],
                                               self.comb[:, ts * E:(ts + 1) * E],
                                               self.idf_sb[:, :])
                        mk_ct.append(cx.step("tensor", "pe",
                                             [("ve", cx.cnt["ve"])], tb))
                    def ct_ev(e):
                        last = None
                        for ts in range(4):
                            last = e.tensor_copy(self.combT[:, ts * 128:(ts + 1) * 128],
                                                 bank[6][0:E, ts * 128:(ts + 1) * 128])
                        return last
                    m_ct = cx.step("vector", "ve", [("pe", mk_ct[-1])], ct_ev)
                    m_crows[ci] = cx.dma(
                        "gpsimd",
                        [(self.rows_s[8 + 8 * ci + ei:9 + 8 * ci + ei, :],
                          self.combT[ei:ei + 1, :]) for ei in range(E)],
                        "xl", [("ve", m_ct)])

                for ei in range(E):
                    m_wgu_a = cx.dma("sync",
                                     [(wgu[dt], self.wguT[ei, dt * 128:(dt + 1) * 128, :])
                                      for dt in range(0, DT, 2)], "wl",
                                     [("mo", self.m_moeT), ("so", so0),
                                      ("pe", prev_gu_pe)])
                    m_wgu_b = cx.dma("gpsimd",
                                     [(wgu[dt], self.wguT[ei, dt * 128:(dt + 1) * 128, :])
                                      for dt in range(1, DT, 2)], "xl",
                                     [("mo", self.m_moeT), ("so", so0),
                                      ("pe", prev_gu_pe)])
                    m_wd = cx.dma("sync",
                                  [(wd[:, i * 2048:(i + 1) * 2048],
                                    self.wdT[ei, :, 2 * i:2 * i + 2, :]
                                    .rearrange("p a d -> p (a d)"))
                                   for i in range(HT // 2)], "wl",
                                  [("mo", self.m_moeT), ("pe", prev_dn_pe),
                                   ("so", so0)])
                    m_cr = {}
                    for ci in range(2):
                        m_cr[ci] = cx.dma(
                            "gpsimd",
                            [(crep[ci],
                              self.rows_s[8 + 8 * ci + ei:9 + 8 * ci + ei, :]
                              .partition_broadcast(128))],
                            "xl", [("xl", m_crows[ci]),
                                   ("ve", crep_free.get(ci, 0))])
                    for ci in range(2):
                        se0 = cx.cnt["se"]
                        ve0 = cx.cnt["ve"]
                        for ht in range(HT):
                            gb = bank[0 + (ht % 2)]
                            ub = bank[2 + (ht % 2)]
                            def g_body(e, ht=ht, gb=gb, ci=ci):
                                last = None
                                for dt in range(DT):
                                    last = e.matmul(gb,
                                                    wgu[dt][:, ht * 128:(ht + 1) * 128],
                                                    hbf[ci][dt], start=(dt == 0),
                                                    stop=(dt == DT - 1))
                                return last
                            w = [("wl", m_wgu_a), ("xl", m_wgu_b), ("ve", m_h[ci])]
                            if ht >= 2:
                                w += [("se", se0 + ht - 1), ("ve", ve0 + ht - 1)]
                            elif ci == 1 or ei > 0 or win > 0:
                                w.append(("se", cx.cnt["se"]))
                                w.append(("ve", cx.cnt["ve"]))
                            m_g = cx.step("tensor", "pe", w, g_body)
                            def u_body(e, ht=ht, ub=ub, ci=ci):
                                last = None
                                for dt in range(DT):
                                    last = e.matmul(
                                        ub,
                                        wgu[dt][:, HID + ht * 128:HID + (ht + 1) * 128],
                                        hbf[ci][dt], start=(dt == 0),
                                        stop=(dt == DT - 1))
                                return last
                            m_gu = cx.step("tensor", "pe", [("ve", m_h[ci])], u_body)
                            def si_body(e, ht=ht, gb=gb):
                                return e.activation(out=silu[ht % 2], in_=gb,
                                                    func=AF.Silu)
                            m_si = cx.step("scalar", "se", [("pe", m_g)], si_body,
                                            barrier=False)
                            def he_body(e, ht=ht, ub=ub):
                                return e.scalar_tensor_tensor(
                                    hebf[ht], ub, 1.0, silu[ht % 2],
                                    ALU.mult, ALU.mult)
                            cx.step("vector", "ve", [("se", m_si), ("pe", m_gu)],
                                    he_body)
                        m_he_all = cx.cnt["ve"]
                        if ci == 1:
                            prev_gu_pe = cx.cnt["pe"]
                        ve1 = cx.cnt["ve"]
                        for dtile in range(DT):
                            db = bank[4 + (dtile % 2)]
                            def dn_body(e, dtile=dtile, db=db):
                                last = None
                                for ht in range(HT):
                                    last = e.matmul(
                                        db,
                                        wd[:, ht * D + dtile * 128:ht * D + (dtile + 1) * 128],
                                        hebf[ht], start=(ht == 0), stop=(ht == HT - 1))
                                return last
                            w = [("wl", m_wd), ("ve", m_he_all)]
                            if dtile >= 2:
                                w.append(("ve", ve1 + dtile - 1))
                            m_dn = cx.step("tensor", "pe", w, dn_body)
                            def ac_body(e, dtile=dtile, db=db, ei=ei, ci=ci):
                                if ei == 0:
                                    return e.scalar_tensor_tensor(
                                        accb[ci][dtile], db, 1.0, crep[ci],
                                        ALU.mult, ALU.mult)
                                e.scalar_tensor_tensor(sq[dtile], db, 1.0,
                                                       crep[ci],
                                                       ALU.mult, ALU.mult)
                                e.drain(fusable=False)
                                return e.tensor_tensor(accb[ci][dtile],
                                                       accb[ci][dtile],
                                                       sq[dtile], ALU.add)
                            cx.step("vector", "ve", [("pe", m_dn), ("xl", m_cr[ci])],
                                    ac_body)
                        if ci == 1:
                            prev_dn_pe = cx.cnt["pe"]
                        crep_free[ci] = cx.cnt["ve"]
                for ci, col in enumerate(cols):
                    m_x2 = self.load_col(cx, col, xc,
                                         [("pe", cx.cnt["pe"]), ("ve", cx.cnt["ve"])])
                    def res_body(e, ci=ci):
                        last = None
                        for dt in range(DT):
                            last = e.tensor_tensor(xc[dt], xc[dt], accb[ci][dt],
                                                   ALU.add)
                        return last
                    m_res = cx.step("vector", "ve", [("xl", m_x2)], res_body)
                    if last_phase:
                        self.transpose_out(cx, col, xc, stage, [("ve", m_res)])
                    else:
                        self.store_col(cx, col, xc, m_res)
            cx.flush()

    def phase_attn_a(self, cx, first_phase, last_phase):
        nc = self.nc
        so0 = cx.cnt["so"]
        with contextlib.ExitStack() as es:
            def sb(name, shape, dtype):
                return es.enter_context(nc.sbuf_tensor(name, shape, dtype))[:, :]
            xc = [sb(f"a_xc{d}", [128, COL], F32) for d in range(DT)]
            h1 = [sb(f"a_h1{d}", [128, COL], F32) for d in range(DT)]
            xa = [sb(f"a_xa{d}", [128, COL], F32) for d in range(DT)]
            rr = [sb(f"a_rr{d}", [128, COL], F32) for d in range(DT)]
            sqt = sb("a_sq", [128, DT * COL], F32)
            sq = [sqt[:, d * COL:(d + 1) * COL] for d in range(DT)]
            stage = [sqt[:, s * 1024:(s + 1) * 1024] for s in range(4)]
            kvx = sqt[:, 0:515]
            khx = sqt[:, 515:1030]
            Ax = sqt[:, 1030:1545]
            Bx = sqt[:, 1545:2060]
            dec2 = sqt[:, 2060:2316]
            k_f = sqt[:, 2316:2828]
            v_f = sqt[:, 2828:3340]
            bc = [sb(f"a_bc{i}", [128, COL], F32) for i in range(4)]
            xs_t = sb("a_xs", [128, COL], F32)
            d1_t = sb("a_d1", [128, COL], F32)
            wk_sc = sb("a_wksc", [128, 20], F32)
            ew = [wk_sc[:, 8 + d:9 + d] for d in range(DT)]
            eu_t = sb("a_eu", [128, DT], F32)
            eu = [eu_t[:, d:d + 1] for d in range(DT)]
            mixk = [sb(f"a_mk{d}", [128, COL], BF16) for d in range(DT)]
            mixv = [sb(f"a_mv{d}", [128, COL], BF16) for d in range(DT)]
            mixr = [sb(f"a_mr{d}", [128, COL], BF16) for d in range(DT)]
            wkvb = [sb(f"a_wk{d}", [128, COL], BF16) for d in range(DT)]
            mixck = [sb(f"a_mc{d}", [128, COL], BF16) for d in range(DT)]
            mixcr = [sb(f"a_mq{d}", [128, COL], BF16) for d in range(DT)]
            kk2t = sb("a_kk2", [128, (FT - 16) * COL], BF16)
            kk2 = (mixk + mixv +
                   [kk2t[:, f * COL:(f + 1) * COL] for f in range(FT - 16)])
            wbuf = sb("a_wbuf", [128, 2 * 3072], BF16)
            wslab = [wbuf[:, 0:3072], wbuf[:, 3072:6144]]
            self.stg_moe = es.enter_context(
                nc.sbuf_tensor("stg_moe", [128, 2 * 2048], BF16,
                               side="right"))[:, :]
            A_s = [self.stv[:, i:i + 1] for i in range(0, 8)]
            B_s = [self.stv[:, i:i + 1] for i in range(8, 16)]
            P_s = [self.stv[:, i:i + 1] for i in range(16, 24)]
            csh = [self.stv[:, i:i + 1] for i in range(24, 32)]
            csh2 = [self.stv[:, i:i + 1] for i in range(32, 40)]
            km = wk_sc[:, 0:1]; Mt = wk_sc[:, 1:2]; negM = wk_sc[:, 2:3]
            dP = wk_sc[:, 3:4]; edP = wk_sc[:, 4:5]; iA = wk_sc[:, 5:6]
            iB = wk_sc[:, 6:7]; lnB = wk_sc[:, 7:8]
            iA2 = wk_sc[:, 16:17]; iB2 = wk_sc[:, 17:18]
            bank = self.bank
            cx.begin()

            def init_v(e):
                for dt in range(DT):
                    e.memset(A_s[dt], 0.0)
                    e.memset(B_s[dt], 0.0)
                    e.memset(P_s[dt], PINIT)
                    e.memset(csh[dt], 0.0)
                    e.memset(csh2[dt], 0.0)
                return e.memset(dec2, 1.0)
            m_iv = cx.step("vector", "ve", [], init_v)
            def init_s(e):
                for dt in range(DT):
                    e.activation(out=ew[dt], in_=self.pv["time_decay"][dt], func=AF.Exp)
                for dt in range(DT):
                    e.activation(out=ew[dt], in_=ew[dt], func=AF.Exp, scale=-1.0)
                last = None
                for dt in range(DT):
                    last = e.activation(out=eu[dt], in_=self.pv["time_first"][dt],
                                        func=AF.Exp)
                return last
            m_is = cx.step("scalar", "se", [("xl", self.m_cvec)], init_s)

            wpar = 0
            wfree = {0: 0, 1: 0}
            for col in range(NCOL):
                ser = [("so", cx.cnt["so"])]
                if first_phase:
                    m_x = self.transpose_in(cx, col, xc, stage, ser)
                    waits_x = [("ve", m_x)]
                else:
                    m_x = self.load_col(cx, col, xc, ser)
                    waits_x = [("xl", m_x)]
                st = self.ln_stats(cx, xc, sq, True, waits_x)
                m_bc = self.bcast_rows(cx,
                                       [(self.row(2), 0, bc[0]), (self.row(3), 1, bc[1]),
                                        (self.row(4), 2, bc[2]), (self.row(5), 3, bc[3])],
                                       [("ve", st["ms"])])
                def hx_body(e):
                    last = None
                    for dt in range(DT):
                        e.tensor_tensor(h1[dt], xc[dt], bc[0], ALU.mult)
                        e.tensor_tensor(h1[dt], h1[dt], bc[1], ALU.subtract)
                        e.tensor_tensor(sq[dt], xc[dt], bc[2], ALU.mult)
                        e.tensor_tensor(sq[dt], sq[dt], bc[3], ALU.subtract)
                        last = e.tensor_scalar(xa[dt], sq[dt],
                                               self.pv["ln0_g"][dt][:, 0:1],
                                               self.pv["ln0_b"][dt][:, 0:1],
                                               ALU.mult, ALU.add)
                    return last
                m_hx = cx.step("vector", "ve", [("xl", m_bc)], hx_body,
                               barrier=False)
                def mix_body(e):
                    last = None
                    for dt in range(DT):
                        e.tensor_copy(xs_t[:, 1:COL], xa[dt][:, 0:COL - 1])
                        e.tensor_copy(xs_t[:, 0:1], csh[dt])
                        e.tensor_copy(csh[dt], xa[dt][:, COL - 1:COL])
                        e.tensor_tensor(d1_t, xa[dt], xs_t, ALU.subtract)
                        for name, dstl in (("tm_k", mixk), ("tm_v", mixv), ("tm_r", mixr)):
                            last = e.scalar_tensor_tensor(
                                dstl[dt], d1_t, self.pv[name][dt][:, 0:1], xs_t,
                                ALU.mult, ALU.add)
                    return last
                m_mix = cx.step("vector", "ve", [("ve", m_hx)], mix_body)
                if col == 0 and getattr(self, "debug", False):
                    cx.dma("gpsimd", [(self.dbg[0:1, :], h1[0][0:1, :]),
                                      (self.dbg[1:2, :], xa[0][0:1, :]),
                                      (self.dbg[2:3, :], mixk[0][0:1, :]),
                                      (self.dbg[3:4, :], mixr[0][0:1, :]),
                                      (self.dbg[30:31, :], bc[0][0:1, :]),
                                      (self.dbg[31:32, :], bc[2][0:1, :])], "so",
                           [("ve", m_mix)])

                for dt2 in range(DT):
                    b = wpar % 2; wpar += 1
                    def mkpairs(dt2=dt2, b=b):
                        ps = []
                        for i, n in enumerate(("Wk", "Wv", "Wr")):
                            ps.append(self.slab_pair(
                                wslab[b][:, i * 1024:(i + 1) * 1024],
                                self.wbf[n], dt2 * 128, (dt2 + 1) * 128))
                        return ps
                    m_w = cx.dma("sync", mkpairs(), "wl",
                                 [("xl", self.m_cast_kvr), ("pe", wfree[b]),
                                  ("so", so0)])
                    kb = bank[0]; vb = bank[1]; rb = bank[2]
                    def kvr_body(e, dt2=dt2, b=b):
                        for j in range(DT):
                            e.matmul(kb, wslab[b][:, j * 128:(j + 1) * 128],
                                     mixk[j], start=(j == 0), stop=(j == DT - 1))
                        for j in range(DT):
                            e.matmul(vb, wslab[b][:, 1024 + j * 128:1024 + (j + 1) * 128],
                                     mixv[j], start=(j == 0), stop=(j == DT - 1))
                        last = None
                        for j in range(DT):
                            last = e.matmul(rb,
                                            wslab[b][:, 2048 + j * 128:2048 + (j + 1) * 128],
                                            mixr[j], start=(j == 0), stop=(j == DT - 1))
                        return last
                    w = [("wl", m_w), ("ve", m_mix)]
                    if dt2 >= 1:
                        w.append(("ve", prev_wkv1))
                        w.append(("se", prev_evr))
                    m_kvr = cx.step("tensor", "pe", w, kvr_body)
                    wfree[b] = m_kvr
                    def ev_r(e, dt2=dt2):
                        return e.activation(out=rr[dt2], in_=rb, func=AF.Sigmoid)
                    m_evr = cx.step("scalar", "se", [("pe", m_kvr)], ev_r,
                                    barrier=False)
                    prev_evr = m_evr
                    def wkv1(e, dt2=dt2):
                        e.tensor_copy(k_f, kb)
                        e.tensor_copy(v_f, vb)
                        e.tensor_reduce(km, k_f, AX.X, ALU.max)
                        e.drain(fusable=False)
                        e.tensor_tensor(Mt, km, P_s[dt2], ALU.max)
                        e.drain(fusable=False)
                        e.tensor_scalar_mul(negM, Mt, -1.0)
                        e.tensor_tensor(dP, P_s[dt2], Mt, ALU.subtract)
                        e.memset(dec2, 1.0)
                        e.drain(fusable=False)
                        return e.tensor_scalar_mul(dec2, dec2, ew[dt2][:, 0:1])
                    m_w1 = cx.step("vector", "ve", [("pe", m_kvr), ("se", m_is)], wkv1,
                                   barrier=False)
                    prev_wkv1 = m_w1
                    def wkv_s(e, dt2=dt2):
                        e.activation(out=khx[:, 1:256], in_=k_f[:, 0:255], func=AF.Exp,
                                     bias=negM[:, 0:1])
                        e.activation(out=khx[:, 257:512], in_=k_f[:, 255:510],
                                     func=AF.Exp, bias=negM[:, 0:1])
                        e.activation(out=khx[:, 513:515], in_=k_f[:, 510:512],
                                     func=AF.Exp, bias=negM[:, 0:1])
                        return e.activation(out=edP, in_=dP, func=AF.Exp)
                    m_ws = cx.step("scalar", "se", [("ve", m_w1)], wkv_s,
                                   barrier=False)
                    def wkv2a(e, dt2=dt2):
                        e.tensor_tensor(kvx[:, 1:256], khx[:, 1:256],
                                        v_f[:, 0:255], ALU.mult)
                        e.tensor_tensor(kvx[:, 257:512], khx[:, 257:512],
                                        v_f[:, 255:510], ALU.mult)
                        e.tensor_tensor(kvx[:, 513:515], khx[:, 513:515],
                                        v_f[:, 510:512], ALU.mult)
                        e.tensor_tensor(iA, A_s[dt2], edP, ALU.mult)
                        e.tensor_tensor(iB, B_s[dt2], edP, ALU.mult)
                        e.drain(fusable=False)
                        e.tensor_copy(kvx[:, 0:1], iA)
                        e.tensor_copy(khx[:, 0:1], iB)
                        e.drain(fusable=False)
                        e.tensor_tensor_scan(out=Ax[:, 0:256], data0=dec2[:, 0:256],
                                             data1=kvx[:, 0:256], initial=0.0,
                                             op0=ALU.mult, op1=ALU.add)
                        e.tensor_tensor_scan(out=Bx[:, 0:256], data0=dec2[:, 0:256],
                                             data1=khx[:, 0:256], initial=0.0,
                                             op0=ALU.mult, op1=ALU.add)
                        e.drain(fusable=False)
                        e.tensor_copy(kvx[:, 256:257], Ax[:, 255:256])
                        e.tensor_copy(khx[:, 256:257], Bx[:, 255:256])
                        e.drain(fusable=False)
                        e.tensor_tensor_scan(out=Ax[:, 256:512], data0=dec2[:, 0:256],
                                             data1=kvx[:, 256:512], initial=0.0,
                                             op0=ALU.mult, op1=ALU.add)
                        e.tensor_tensor_scan(out=Bx[:, 256:512], data0=dec2[:, 0:256],
                                             data1=khx[:, 256:512], initial=0.0,
                                             op0=ALU.mult, op1=ALU.add)
                        e.drain(fusable=False)
                        e.tensor_copy(kvx[:, 512:513], Ax[:, 511:512])
                        e.tensor_copy(khx[:, 512:513], Bx[:, 511:512])
                        e.drain(fusable=False)
                        e.tensor_tensor_scan(out=Ax[:, 512:515], data0=dec2[:, 0:3],
                                             data1=kvx[:, 512:515], initial=0.0,
                                             op0=ALU.mult, op1=ALU.add)
                        return e.tensor_tensor_scan(out=Bx[:, 512:515],
                                             data0=dec2[:, 0:3],
                                             data1=khx[:, 512:515], initial=0.0,
                                             op0=ALU.mult, op1=ALU.add)
                    m_w2a = cx.step("vector", "ve", [("se", m_ws)], wkv2a,
                                    barrier=False)
                    def cvcc(e, dt2=dt2):
                        e.activation(out=v_f[:, 0:255], in_=kvx[:, 1:256],
                                     func=AF.Copy, scale=eu[dt2][:, 0:1])
                        e.activation(out=v_f[:, 255:510], in_=kvx[:, 257:512],
                                     func=AF.Copy, scale=eu[dt2][:, 0:1])
                        e.activation(out=v_f[:, 510:512], in_=kvx[:, 513:515],
                                     func=AF.Copy, scale=eu[dt2][:, 0:1])
                        e.activation(out=k_f[:, 0:255], in_=khx[:, 1:256],
                                     func=AF.Copy, scale=eu[dt2][:, 0:1])
                        e.activation(out=k_f[:, 255:510], in_=khx[:, 257:512],
                                     func=AF.Copy, scale=eu[dt2][:, 0:1])
                        return e.activation(out=k_f[:, 510:512], in_=khx[:, 513:515],
                                            func=AF.Copy, scale=eu[dt2][:, 0:1])
                    m_cvcc = cx.step("scalar", "se", [("ve", m_w2a)], cvcc,
                                     barrier=False)
                    def wkv2b(e, dt2=dt2):
                        e.drain(fusable=False)
                        e.tensor_tensor(kvx[:, 0:255], Ax[:, 0:255], v_f[:, 0:255],
                                        ALU.add)
                        e.tensor_tensor(kvx[:, 255:510], Ax[:, 256:511],
                                        v_f[:, 255:510], ALU.add)
                        e.tensor_tensor(kvx[:, 510:512], Ax[:, 512:514],
                                        v_f[:, 510:512], ALU.add)
                        e.tensor_tensor(khx[:, 0:255], Bx[:, 0:255], k_f[:, 0:255],
                                        ALU.add)
                        e.tensor_tensor(khx[:, 255:510], Bx[:, 256:511],
                                        k_f[:, 255:510], ALU.add)
                        e.tensor_tensor(khx[:, 510:512], Bx[:, 512:514],
                                        k_f[:, 510:512], ALU.add)
                        e.drain(fusable=False)
                        e.reciprocal(v_f, khx[:, 0:512])
                        e.drain(fusable=False)
                        e.tensor_tensor(k_f, kvx[:, 0:512], v_f, ALU.mult)
                        e.tensor_copy(iB, Bx[:, 514:515])
                        e.tensor_copy(iA, Ax[:, 514:515])
                        e.drain(fusable=False)
                        e.tensor_tensor(wkvb[dt2], k_f, rr[dt2], ALU.mult)
                        return e.reciprocal(km, iB)
                    m_w2 = cx.step("vector", "ve", [("se", m_cvcc)], wkv2b)
                    if col == 0 and dt2 == 0 and getattr(self, "debug", False):
                        cx.dma("gpsimd", [(self.dbg[5:6, :], k_f[0:1, :]),
                                          (self.dbg[6:7, :], v_f[0:1, :]),
                                          (self.dbg[8:9, :], Ax[0:1, 0:512]),
                                          (self.dbg[9:10, :], Bx[0:1, 0:512]),
                                          (self.dbg[10:11, :], wkvb[0][0:1, :]),
                                          (self.dbg[13:14, 0:16], wk_sc[0:1, 0:16])], "so",
                               [("ve", m_w2)])
                    def wkv_s2(e, dt2=dt2):
                        return e.activation(out=lnB, in_=iB, func=AF.Ln)
                    m_ws2 = cx.step("scalar", "se", [("ve", m_w2)], wkv_s2,
                                    barrier=False)
                    def wkv3(e, dt2=dt2):
                        e.tensor_tensor(A_s[dt2], iA, km, ALU.mult)
                        e.memset(B_s[dt2], 1.0)
                        return e.tensor_tensor(P_s[dt2], Mt, lnB, ALU.add)
                    cx.step("vector", "ve", [("se", m_ws2)], wkv3, barrier=False)

                m_wkv_all = cx.cnt["ve"]
                for dt2 in range(DT):
                    b = wpar % 2; wpar += 1
                    m_w = cx.dma("sync",
                                 [self.slab_pair(wslab[b][:, 0:1024],
                                                 self.wbf["Wo"], dt2 * 128,
                                                 (dt2 + 1) * 128)], "wl",
                                 [("xl", self.m_cast_attn), ("pe", wfree[b]),
                                  ("so", so0)])
                    ob = bank[3 + dt2 % 2]
                    def wo_body(e, dt2=dt2, b=b, ob=ob):
                        last = None
                        for j in range(DT):
                            last = e.matmul(ob, wslab[b][:, j * 128:(j + 1) * 128],
                                            wkvb[j], start=(j == 0), stop=(j == DT - 1))
                        return last
                    w = [("wl", m_w), ("ve", m_wkv_all)]
                    if dt2 >= 2:
                        w.append(("ve", cx.cnt["ve"]))
                    m_wo = cx.step("tensor", "pe", w, wo_body)
                    wfree[b] = m_wo
                    def y1_body(e, dt2=dt2, ob=ob):
                        return e.tensor_tensor(h1[dt2], h1[dt2], ob, ALU.add)
                    cx.step("vector", "ve", [("pe", m_wo)], y1_body)
                m_y1 = cx.cnt["ve"]
                if col == 0 and getattr(self, "debug", False):
                    cx.dma("gpsimd", [(self.dbg[14:15, :], rr[0][0:1, :]),
                                      (self.dbg[15:16, :], h1[0][0:1, :])], "so",
                           [("ve", m_y1)])

                st2 = self.ln_stats(cx, h1, sq, False, [("ve", m_y1)])
                m_bc2 = self.bcast_rows(cx, [(self.row(2), 4, bc[0]), (self.row(3), 5, bc[1])],
                                        [("ve", st2["ms"])])
                def xc2_body(e):
                    last = None
                    for dt in range(DT):
                        e.tensor_tensor(sq[dt], h1[dt], bc[0], ALU.mult)
                        e.tensor_tensor(sq[dt], sq[dt], bc[1], ALU.subtract)
                        last = e.tensor_scalar(xa[dt], sq[dt],
                                               self.pv["ln1_g"][dt][:, 0:1],
                                               self.pv["ln1_b"][dt][:, 0:1],
                                               ALU.mult, ALU.add)
                    return last
                m_xc2 = cx.step("vector", "ve", [("xl", m_bc2)], xc2_body,
                                barrier=False)
                def mix2_body(e):
                    last = None
                    for dt in range(DT):
                        e.tensor_copy(xs_t[:, 1:COL], xa[dt][:, 0:COL - 1])
                        e.tensor_copy(xs_t[:, 0:1], csh2[dt])
                        e.tensor_copy(csh2[dt], xa[dt][:, COL - 1:COL])
                        e.tensor_tensor(d1_t, xa[dt], xs_t, ALU.subtract)
                        for name, dstl in (("cm_k", mixck), ("cm_r", mixcr)):
                            last = e.scalar_tensor_tensor(
                                dstl[dt], d1_t, self.pv[name][dt][:, 0:1], xs_t,
                                ALU.mult, ALU.add)
                    return last
                m_mix2 = cx.step("vector", "ve", [("ve", m_xc2)], mix2_body)

                for ft in range(FT):
                    b = wpar % 2; wpar += 1
                    m_w = cx.dma("sync",
                                 [self.slab_pair(wslab[b][:, 0:1024],
                                                 self.wbf["Wck"], ft * 128,
                                                 (ft + 1) * 128)], "wl",
                                 [("xl", self.m_cast_attn), ("pe", wfree[b]),
                                  ("so", so0)])
                    cb = bank[0 + ft % 2]
                    def ck_body(e, ft=ft, b=b, cb=cb):
                        last = None
                        for j in range(DT):
                            last = e.matmul(cb, wslab[b][:, j * 128:(j + 1) * 128],
                                            mixck[j], start=(j == 0), stop=(j == DT - 1))
                        return last
                    w = [("wl", m_w), ("ve", m_mix2)]
                    if ft >= 2:
                        w.append(("ve", cx.cnt["ve"]))
                        w.append(("se", cx.cnt["se"]))
                    m_ck = cx.step("tensor", "pe", w, ck_body)
                    wfree[b] = m_ck
                    def rl_body(e, ft=ft, cb=cb):
                        return e.activation(out=d1_t, in_=cb, func=AF.Relu)
                    m_rl = cx.step("scalar", "se",
                                   [("pe", m_ck), ("ve", cx.cnt["ve"])], rl_body,
                                   barrier=False)
                    def k2_body(e, ft=ft, cb=cb):
                        return e.tensor_tensor(kk2[ft], cb, d1_t, ALU.mult)
                    cx.step("vector", "ve", [("se", m_rl)], k2_body)
                m_kk2 = cx.cnt["ve"]

                for dt2 in range(DT):
                    b = wpar % 2; wpar += 1
                    m_w = cx.dma("sync",
                                 [self.slab_pair(wslab[b][:, 0:1024],
                                                 self.wbf["Wcr"], dt2 * 128,
                                                 (dt2 + 1) * 128)], "wl",
                                 [("xl", self.m_cast_attn), ("pe", wfree[b]),
                                  ("so", so0)])
                    qb = bank[2 + dt2 % 2]
                    def cr_body(e, dt2=dt2, b=b, qb=qb):
                        last = None
                        for j in range(DT):
                            last = e.matmul(qb, wslab[b][:, j * 128:(j + 1) * 128],
                                            mixcr[j], start=(j == 0), stop=(j == DT - 1))
                        return last
                    w = [("wl", m_w), ("ve", m_mix2)]
                    if dt2 >= 2:
                        w.append(("se", cx.cnt["se"]))
                    m_cr2 = cx.step("tensor", "pe", w, cr_body)
                    wfree[b] = m_cr2
                    def sg_body(e, dt2=dt2, qb=qb):
                        return e.activation(out=rr[dt2], in_=qb, func=AF.Sigmoid)
                    cx.step("scalar", "se", [("pe", m_cr2)], sg_body, barrier=False)
                m_rr = cx.cnt["se"]

                for dt2 in range(DT):
                    bs = []
                    mws = []
                    for half in range(2):
                        b = wpar % 2; wpar += 1
                        mws.append(cx.dma(
                            "sync",
                            [self.slab_pair(wslab[b][:, 0:2048],
                                            self.wbf["Wcv"][half * 2048:(half + 1) * 2048, :],
                                            dt2 * 128, (dt2 + 1) * 128)], "wl",
                            [("xl", self.m_cast_attn), ("pe", wfree[b]),
                             ("so", so0)]))
                        bs.append(b)
                    vb2 = bank[4 + dt2 % 2]
                    def cv_body(e, dt2=dt2, bs=tuple(bs), vb2=vb2):
                        last = None
                        for j in range(FT):
                            sl = wslab[bs[j // 16]]
                            last = e.matmul(vb2, sl[:, (j % 16) * 128:(j % 16 + 1) * 128],
                                            kk2[j], start=(j == 0), stop=(j == FT - 1))
                        return last
                    w = [("wl", mws[1]), ("ve", m_kk2), ("se", m_rr)]
                    if dt2 >= 2:
                        w.append(("ve", cx.cnt["ve"]))
                    m_cv = cx.step("tensor", "pe", w, cv_body)
                    wfree[bs[0]] = m_cv
                    wfree[bs[1]] = m_cv
                    def fin_body(e, dt2=dt2, vb2=vb2):
                        e.scalar_tensor_tensor(d1_t, vb2, 1.0, rr[dt2], ALU.mult, ALU.mult)
                        e.drain(fusable=False)
                        e.tensor_tensor(d1_t, d1_t, h1[dt2], ALU.add)
                        return e.tensor_tensor(xc[dt2], xc[dt2], d1_t, ALU.add)
                    cx.step("vector", "ve", [("pe", m_cv)], fin_body)
                m_fin = cx.cnt["ve"]
                if col == 0 and getattr(self, "debug", False):
                    cx.dma("gpsimd", [(self.dbg[16:17, :], xa[0][0:1, :]),
                                      (self.dbg[17:18, :], kk2[16][0:1, :]),
                                      (self.dbg[18:19, :], rr[0][0:1, :]),
                                      (self.dbg[19:20, :], xc[0][0:1, :])], "so",
                           [("ve", m_fin)])
                if last_phase:
                    self.transpose_out(cx, col, xc, stage, [("ve", m_fin)])
                else:
                    self.store_col(cx, col, xc, m_fin)
                self.emit_cast(cx, 2)
                if col >= 2:
                    self.emit_moet(cx, 32 if col in (2, NCOL - 2) else 64)
            cx.flush()

    def phase_attn_b(self, cx, first_phase, last_phase):
        nc = self.nc
        so0 = cx.cnt["so"]
        with contextlib.ExitStack() as es:
            def sb(name, shape, dtype):
                return es.enter_context(nc.sbuf_tensor(name, shape, dtype))[:, :]
            xc = [sb(f"b_xc{d}", [128, COL], F32) for d in range(DT)]
            sqt = sb("b_sq", [128, DT * COL], F32)
            sq = [sqt[:, d * COL:(d + 1) * COL] for d in range(DT)]
            stage = [sqt[:, s * 1024:(s + 1) * 1024] for s in range(4)]
            bc = [sb(f"b_bc{i}", [128, COL], F32) for i in range(4)]
            rrep = sb("b_rrep", [64, COL], F32)
            negmh = sb("b_negmh", [128, HEADS], F32)
            m4 = sb("b_m4", [128, 20], F32)
            hbf = [sb(f"b_hb{d}", [128, COL], BF16) for d in range(DT)]
            xnbf = [sb(f"b_xn{d}", [128, COL], BF16) for d in range(DT)]
            qh = [sb(f"b_q{h}", [64, COL], BF16) for h in range(HEADS)]
            Kc = sb("b_Kc", [64, T], BF16)
            Vtok = sb("b_Vt", [128, 32 * (DH + 1)], BF16)
            vt_bf = sb("b_vtb", [64, COL], BF16)
            ao = [sb(f"b_ao{d}", [128, COL], BF16) for d in range(DT)]
            Pt = [sb(f"b_P{i}", [128, 4 * COL], BF16) for i in range(2)]
            mask01 = sb("b_msk", [128, 4 * COL], BF16)
            wbuf = sb("b_wbuf", [128, 2 * 1024], BF16)
            wslab = [wbuf[:, 0:1024], wbuf[:, 1024:2048]]
            wkv_sb = sb("b_wkv", [128, DT * 128], BF16)
            bank = self.bank
            bankb = self.bankb
            cx.begin()

            m_msk = cx.dma("sync", [(mask01[:, :], self.c_mask01[...])], "wl",
                           [("so", so0)])
            m_wkvw = cx.dma("sync",
                            [self.slab_pair(wkv_sb[:, :], self.wbf["Wkv"], 0, 128)],
                            "wl", [("xl", self.m_cast_attn), ("so", so0)])

            wpar = 0
            wfree = {0: 0, 1: 0}
            for col in range(NCOL):
                ser = [("so", cx.cnt["so"])]
                if first_phase:
                    m_x = self.transpose_in(cx, col, xc, stage, ser)
                    waits_x = [("ve", m_x)]
                else:
                    m_x = self.load_col(cx, col, xc, ser)
                    waits_x = [("xl", m_x)]
                st = self.ln_stats(cx, xc, sq, True, waits_x)
                m_bc = self.bcast_rows(cx,
                                       [(self.row(2), 0, bc[0]), (self.row(3), 1, bc[1]),
                                        (self.row(4), 2, bc[2]), (self.row(5), 3, bc[3])],
                                       [("ve", st["ms"])])
                def hx_body(e):
                    last = None
                    for dt in range(DT):
                        e.tensor_tensor(sq[dt], xc[dt], bc[0], ALU.mult)
                        e.tensor_tensor(hbf[dt], sq[dt], bc[1], ALU.subtract)
                        e.tensor_tensor(sq[dt], xc[dt], bc[2], ALU.mult)
                        e.tensor_tensor(sq[dt], sq[dt], bc[3], ALU.subtract)
                        last = e.tensor_scalar_mul(xnbf[dt], sq[dt],
                                                   self.pv["gamma"][dt][:, 0:1])
                    return last
                m_h = cx.step("vector", "ve", [("xl", m_bc)], hx_body)
                if col == 0 and getattr(self, "debug", False):
                    cx.dma("gpsimd", [(self.dbg[0:1, :], hbf[0][0:1, :]),
                                      (self.dbg[1:2, :], xnbf[0][0:1, :])], "so",
                           [("ve", m_h)])

                for dt2 in range(DT):
                    b = wpar % 2; wpar += 1
                    m_w = cx.dma("sync",
                                 [self.slab_pair(wslab[b][:, 0:1024],
                                                 self.wbf["Wq"], dt2 * 128,
                                                 (dt2 + 1) * 128)], "wl",
                                 [("xl", self.m_cast_attn), ("pe", wfree[b]),
                                  ("so", so0)])
                    qb = bank[0 + dt2 % 2]
                    def q_body(e, dt2=dt2, b=b, qb=qb):
                        last = None
                        for j in range(DT):
                            last = e.matmul(qb, wslab[b][:, j * 128:(j + 1) * 128],
                                            xnbf[j], start=(j == 0), stop=(j == DT - 1))
                        return last
                    w = [("wl", m_w), ("ve", m_h)]
                    if dt2 >= 2:
                        w.append(("se", cx.cnt["se"]))
                    m_q = cx.step("tensor", "pe", w, q_body)
                    wfree[b] = m_q
                    def qe_body(e, dt2=dt2, qb=qb):
                        e.activation(out=qh[2 * dt2][:, :], in_=qb[0:64, :],
                                     func=AF.Copy, scale=SCALE_B)
                        return e.activation(out=qh[2 * dt2 + 1][:, :], in_=qb[64:128, :],
                                            func=AF.Copy, scale=SCALE_B)
                    cx.step("scalar", "se", [("pe", m_q)], qe_body)
                m_qdone = cx.cnt["se"]
                def kv_body(e):
                    last = None
                    for j in range(DT):
                        last = e.matmul(bank[6], wkv_sb[:, j * 128:(j + 1) * 128],
                                        hbf[j], start=(j == 0), stop=(j == DT - 1))
                    return last
                m_kv = cx.step("tensor", "pe",
                               [("wl", m_wkvw), ("ve", m_h), ("se", m_qdone)], kv_body)
                def kc_body(e, col=col):
                    if col == 0 and getattr(self, "debug", False):
                        e.tensor_copy(ptmp[0], bank[6])
                    e.tensor_copy(Kc[:, col * COL:(col + 1) * COL], bank[6][0:64, :])
                    return e.tensor_copy(vt_bf[:, :], bank[6][64:128, :])
                m_kc = cx.step("vector", "ve", [("pe", m_kv)], kc_body)
                mk_vt = []
                for i in range(4):
                    def tb(e, i=i):
                        return e.transpose(bankb[:, i * 64:(i + 1) * 64],
                                           vt_bf[:, i * 128:(i + 1) * 128],
                                           self.idb_sb[0:64, 0:64])
                    w = [("ve", m_kc)]
                    mk_vt.append(cx.step("tensor", "pe", w, tb))
                def vt_ev(e, col=col):
                    last = None
                    for i in range(4):
                        kb = col * 4 + i
                        e.tensor_copy(Vtok[:, kb * 65:kb * 65 + 64],
                                      bankb[:, i * 64:(i + 1) * 64])
                        last = e.memset(Vtok[:, kb * 65 + 64:kb * 65 + 65], 1.0)
                    return last
                m_vt = cx.step("vector", "ve", [("pe", mk_vt[-1])], vt_ev)
                if col == 0 and getattr(self, "debug", False):
                    cx.dma("gpsimd", [(self.dbg[2:3, :], Kc[0:1, 0:COL]),
                                      (self.dbg[3:4, :], qh[0][0:1, :]),
                                      (self.dbg[4:5, 0:130], Vtok[0:1, 0:130]),
                                      (self.dbg[13:14, :], ptmp[0][0:1, :]),
                                      (self.dbg[14:15, 0:128], self.wbf["Wkv"][0:1, :]),
                                      (self.dbg[15:16, 0:128], wkv_sb[0:1, 0:128])], "so",
                           [("ve", m_vt), ("se", m_qdone)])

                nkc = 4 * (col + 1)
                def ksq_body(e, col=col):
                    return e.activation(out=sq[0][0:64, :],
                                        in_=Kc[:, col * COL:(col + 1) * COL],
                                        func=AF.Square)
                m_ksq = cx.step("scalar", "se", [("ve", m_kc)], ksq_body,
                                barrier=False)
                def ksum_body(e):
                    return e.matmul(bank[5][0:1, :], self.ones_sb[0:64, :],
                                    sq[0][0:64, :], start=True, stop=True)
                m_ksum = cx.step("tensor", "pe",
                                 [("se", m_ksq), ("ve", m_vt),
                                  ("wl", self.m_fixed)], ksum_body)
                def kmax_body(e, col=col):
                    e.tensor_reduce(m4[:, 16:17][0:1, :], bank[5][0:1, :],
                                    AX.X, ALU.max)
                    e.drain(fusable=False)
                    if col == 0:
                        e.tensor_copy(m4[:, 17:18][0:1, :],
                                      m4[:, 16:17][0:1, :])
                    else:
                        e.tensor_tensor(m4[:, 17:18][0:1, :],
                                        m4[:, 17:18][0:1, :],
                                        m4[:, 16:17][0:1, :], ALU.max)
                    e.drain(fusable=False)
                    return e.tensor_scalar_mul(m4[:, 18:19][0:1, :],
                                               m4[:, 17:18][0:1, :], 0.0625)
                m_kmax = cx.step("vector", "ve", [("pe", m_ksum)], kmax_body)
                def qsq_body(e):
                    last = None
                    for h in range(HEADS):
                        last = e.activation(
                            out=sq[h // 2][(h % 2) * 64:(h % 2) * 64 + 64, :],
                            in_=qh[h][:, :], func=AF.Square)
                    return last
                m_qsq = cx.step("scalar", "se", [("se", m_qdone)], qsq_body,
                                barrier=False)
                grp_mark = {}
                for g in range(4):
                    def qsum_body(e, g=g):
                        last = None
                        for j in range(4):
                            h = 4 * g + j
                            p0 = (h % 2) * 64
                            last = e.matmul(
                                bank[3 + j][0:1, :],
                                self.ones_sb[p0:p0 + 64, :],
                                sq[h // 2][p0:p0 + 64, :],
                                start=True, stop=True)
                        return last
                    w = [("se", m_qsq), ("ve", m_kmax)]
                    if g >= 1:
                        w.append(("ve", grp_mark[g - 1]))
                    m_qs = cx.step("tensor", "pe", w, qsum_body)
                    def qmax_body(e, g=g):
                        last = None
                        for j in range(4):
                            h = 4 * g + j
                            last = e.tensor_reduce(m4[:, h:h + 1][0:1, :],
                                                   bank[3 + j][0:1, :],
                                                   AX.X, ALU.max)
                        return last
                    grp_mark[g] = cx.step("vector", "ve", [("pe", m_qs)],
                                          qmax_body)
                def mcomb_body(e):
                    e.tensor_scalar(m4[:, 0:16][0:1, :], m4[:, 0:16][0:1, :],
                                    4.0, m4[0:1, 18:19], ALU.mult, ALU.add)
                    e.drain(fusable=False)
                    return e.tensor_scalar_mul(m4[:, 0:16][0:1, :],
                                               m4[:, 0:16][0:1, :], -1.0)
                m_mn = cx.step("vector", "ve", [("ve", grp_mark[3])], mcomb_body)
                m_negmb = self.bcast_rows_n(
                    cx, [(m4[0:1, 0:16], 24, negmh[:, 0:16], 16)],
                    [("ve", m_mn)])

                prev_aw = 0
                for h in range(HEADS):
                    av = bank[4]
                    nq4 = (nkc + 3) // 4
                    exp_marks = {}
                    av_marks = {}
                    for q4 in range(nq4):
                        kcs = [kc for kc in range(4 * q4, min(4 * q4 + 4, nkc))]
                        def qk_body(e, h=h, kcs=tuple(kcs)):
                            last = None
                            for kc in kcs:
                                last = e.matmul(
                                    self.psA[:, (kc % 4) * 512:(kc % 4 + 1) * 512],
                                    Kc[:, kc * 128:(kc + 1) * 128],
                                    qh[h][:, :], start=True, stop=True)
                            return last
                        w = [("xl", m_negmb)]
                        if q4 >= 1:
                            w.append(("se", exp_marks[q4 - 1]))
                        elif h > 0:
                            w.append(("se", cx.cnt["se"]))
                        m_qk = cx.step("tensor", "pe", w, qk_body)
                        wdt = 512 * len(kcs)
                        def exp_body(e, q4=q4, wdt=wdt, h=h):
                            return e.activation(out=Pt[q4 % 2][:, 0:wdt],
                                                in_=self.psA[:, 0:wdt],
                                                func=AF.Exp,
                                                bias=negmh[:, h:h + 1])
                        w = [("ve", cx.cnt["ve"]), ("pe", m_qk)]
                        if q4 >= 2:
                            w.append(("pe", av_marks[q4 - 2]))
                        exp_marks[q4] = cx.step("scalar", "se", w, exp_body)
                        msk_marks = []
                        for kc in kcs:
                            if kc >= 4 * col:
                                mi = kc - 4 * col
                                def msk_body(e, q4=q4, kc=kc, mi=mi):
                                    sl = Pt[q4 % 2][:, (kc % 4) * 512:
                                                    (kc % 4 + 1) * 512]
                                    return e.tensor_tensor(
                                        sl, sl, mask01[:, mi * COL:(mi + 1) * COL],
                                        ALU.mult)
                                msk_marks.append(cx.step(
                                    "vector", "ve", [("se", exp_marks[q4])],
                                    msk_body))
                        av_w = [("se", exp_marks[q4])]
                        if msk_marks:
                            av_w.append(("ve", msk_marks[-1]))
                        if q4 == 0:
                            av_w.append(("ve", prev_aw))
                        def av_body(e, h=h, q4=q4, kcs=tuple(kcs), nkc=nkc):
                            last = None
                            for kc in kcs:
                                last = e.matmul(
                                    av[0:65, :], Vtok[:, kc * 65:kc * 65 + 65],
                                    Pt[q4 % 2][:, (kc % 4) * 512:(kc % 4) * 512 + 512],
                                    start=(kc == 0), stop=(kc == nkc - 1))
                            return last
                        av_marks[q4] = cx.step("tensor", "pe", av_w, av_body)
                    def rec_body(e, h=h):
                        return e.reciprocal(self.rown(7, COL), av[64:65, :])
                    m_rc = cx.step("vector", "ve", [("pe", av_marks[nq4 - 1])],
                                   rec_body)
                    def rep_body(e, h=h):
                        return e.matmul(bank[6][0:64, :], self.onesrow[0:1, 0:64],
                                        self.rown(7, COL), start=True, stop=True)
                    m_rep = cx.step("tensor", "pe",
                                    [("ve", m_rc), ("ve", self.m_onesrow)], rep_body)
                    def rcp_body(e, h=h):
                        return e.tensor_copy(rrep[:, :], bank[6][0:64, :])
                    m_rcp = cx.step("vector", "ve", [("pe", m_rep)], rcp_body)
                    def aw_body(e, h=h):
                        return e.tensor_tensor(ao[h // 2][(h % 2) * 64:(h % 2) * 64 + 64, :],
                                               av[0:65, :][0:64, :], rrep[:, :],
                                               ALU.mult)
                    prev_aw = cx.step("vector", "ve", [("ve", m_rcp)], aw_body)
                m_att = cx.cnt["ve"]

                for dt2 in range(DT):
                    b = wpar % 2; wpar += 1
                    m_w = cx.dma("sync",
                                 [self.slab_pair(wslab[b][:, 0:1024],
                                                 self.wbf["Wob"], dt2 * 128,
                                                 (dt2 + 1) * 128)], "wl",
                                 [("pe", wfree[b]), ("so", so0)])
                    ob = bank[0 + dt2 % 2]
                    def wo_body(e, dt2=dt2, b=b, ob=ob):
                        last = None
                        for j in range(DT):
                            last = e.matmul(ob, wslab[b][:, j * 128:(j + 1) * 128],
                                            ao[j], start=(j == 0), stop=(j == DT - 1))
                        return last
                    w = [("wl", m_w), ("ve", m_att)]
                    if dt2 >= 2:
                        w.append(("ve", cx.cnt["ve"]))
                    m_wo = cx.step("tensor", "pe", w, wo_body)
                    wfree[b] = m_wo
                    def res_body(e, dt2=dt2, ob=ob):
                        return e.tensor_tensor(xc[dt2], xc[dt2], ob, ALU.add)
                    cx.step("vector", "ve", [("pe", m_wo)], res_body)
                m_fin = cx.cnt["ve"]
                if last_phase:
                    self.transpose_out(cx, col, xc, stage, [("ve", m_fin)])
                else:
                    self.store_col(cx, col, xc, m_fin)
            cx.flush()

    def build(self):
        nc = bass.Bass("TRN2", target_bir_lowering=False)
        self.nc = nc
        self.declare(nc)
        es = contextlib.ExitStack()
        with es:
            blk = es.enter_context(nc.Block())
            cx = Cx(nc, es, blk)
            self.alloc_fixed(nc, es)
            self.preamble(cx, es)
            first = True
            for ph in self.phases:
                last = ph == self.phases[-1]
                if ph in ("f1", "f2"):
                    self.phase_ffn(cx, first, last)
                elif ph == "a":
                    self.phase_attn_a(cx, first, last)
                else:
                    self.phase_attn_b(cx, first, last)
                first = False
            total_so = cx.cnt["so"]
            cx.begin()
            def fin(q):
                q.wait_ge(cx.sem["so"], total_so)
            cx.emit("gpsimd", fin)
            cx.flush()
        return nc


def make_in_maps(inputs):
    x = np.ascontiguousarray(inputs["L0_attn_a_x"])
    base = {
        "ln0_g": inputs["L0_attn_a_ln0_g"], "ln0_b": inputs["L0_attn_a_ln0_b"],
        "ln1_g": inputs["L0_attn_a_ln1_g"], "ln1_b": inputs["L0_attn_a_ln1_b"],
        "time_decay": inputs["L0_attn_a_time_decay"],
        "time_first": inputs["L0_attn_a_time_first"],
        "tm_k": inputs["L0_attn_a_tm_k"], "tm_v": inputs["L0_attn_a_tm_v"],
        "tm_r": inputs["L0_attn_a_tm_r"],
        "Wk": inputs["L0_attn_a_Wk"], "Wv": inputs["L0_attn_a_Wv"],
        "Wr": inputs["L0_attn_a_Wr"], "Wo": inputs["L0_attn_a_Wo"],
        "cm_k": inputs["L0_attn_a_cm_k"], "cm_r": inputs["L0_attn_a_cm_r"],
        "Wck": inputs["L0_attn_a_Wck"], "Wcv": inputs["L0_attn_a_Wcv"],
        "Wcr": inputs["L0_attn_a_Wcr"],
        "gamma": inputs["L0_attn_b_gamma"], "Wq": inputs["L0_attn_b_Wq"],
        "Wkv": inputs["L0_attn_b_Wkv"], "Wob": inputs["L0_attn_b_Wo"],
        "router_w": inputs["L0_ffn_router_w"],
        "w_gate": inputs["L0_ffn_w_gate"], "w_up": inputs["L0_ffn_w_up"],
        "w_down": inputs["L0_ffn_w_down"],
    }
    base = {k: np.ascontiguousarray(v) for k, v in base.items()}
    in_maps = []
    for c in range(NCORES):
        m = dict(base)
        m["x"] = x[c]
        in_maps.append(m)
    return in_maps


def kernel(**inputs):
    nc = Build().build()
    in_maps = make_in_maps(inputs)
    res = run_bass_kernel_spmd(nc, in_maps, core_ids=list(range(NCORES)))
    return np.stack([res.results[c]["y"] for c in range(NCORES)], axis=0)
